# version 07
# baseline (speedup 1.0000x reference)
import contextlib
import numpy as np
import ml_dtypes
import concourse.bass as bass
import concourse.mybir as mybir
from concourse.bass_utils import run_bass_kernel_spmd

F32 = mybir.dt.float32
BF16 = mybir.dt.bfloat16
AF = mybir.ActivationFunctionType
ALU = mybir.AluOpType
AX = mybir.AxisListType
BF16NP = ml_dtypes.bfloat16

LN2_INV = float(np.float32(1.4426950408889634))
NEG_BIG = -1.0e30
NCORES = 8

CFG = dict(B=2, S=4096, HID=2048, I=8192, QLR=1536, KVLR=512, H=16,
           DN=128, DR=64, DV=128)


class Sem:

    def __init__(self, handle):
        self.h = handle
        self.v = 0

    def inc(self, instr, n=1):
        instr.then_inc(self.h, n)
        self.v += n
        return self.v


class EC:

    def __init__(self, eng, sem):
        self.e = eng
        self.s = sem

    def step(self, instr):
        self.s.inc(instr)
        self.e.wait_ge(self.s.h, self.s.v)
        return self.s.v

    def mark(self, instr, n=1):
        return self.s.inc(instr, n)

    def tok(self):
        return (self.s, self.s.v)

    def wait(self, tok):
        sem, v = tok
        if v > 0:
            self.e.wait_ge(sem.h, v)


class Prog:

    def __init__(self, nc, es):
        self.nc = nc
        self.es = es
        self.V = EC(nc.vector, self.sem("semV"))
        self.G = EC(nc.gpsimd, self.sem("semG"))
        self.SC = EC(nc.scalar, self.sem("semSC"))
        self.T = EC(nc.tensor, self.sem("semT"))
        self.SY = EC(nc.sync, self.sem("semSY"))
        self.dma_slots = {
            q: [Sem(self.es.enter_context(nc.semaphore(f"dma_{q}{i}"))) for i in range(4)]
            for q in ("sy", "sc", "g")
        }
        self.dma_idx = {q: 0 for q in ("sy", "sc", "g")}

    def sem(self, name):
        return Sem(self.es.enter_context(self.nc.semaphore(name)))

    def sb(self, name, shape, dtype=F32):
        return self.es.enter_context(self.nc.sbuf_tensor(name, shape, dtype))

    def ps(self, name, shape, dtype=F32):
        return self.es.enter_context(self.nc.psum_tensor(name, shape, dtype))

    def make_two(self, maxnb):
        self.G.two = self.sb("g_two", [128, maxnb])
        self.G.step(self.G.e.memset(self.G.two[:, :], 2.0))

    def make_ident(self):
        self.ident = self.sb("ident_t", [128, 128], BF16)
        g = self.G
        g.step(g.e.memset(self.ident[:, :], 1.0))
        g.step(g.e.affine_select(self.ident[:, :], self.ident[:, :], [[1, 128]],
                                 ALU.is_equal, 0.0, base=0, channel_multiplier=-1))
        return g.tok()

    def load(self, dst_ap, src_ap, queue="sy"):
        eng = {"sy": self.nc.sync, "sc": self.nc.scalar, "g": self.nc.gpsimd}[queue]
        i = self.dma_idx[queue]
        self.dma_idx[queue] = (i + 1) % 4
        sem = self.dma_slots[queue][i]
        if sem.v > 0:
            eng.wait_ge(sem.h, sem.v)
        sem.inc(eng.dma_start(out=dst_ap, in_=src_ap), 16)
        return (sem, sem.v)


def quant_scratch(pr, tag, w):
    nb = w // 32
    z = {"w": w}
    for nm in ("z0", "z1", "z2", "z3", "z4", "z5"):
        z[nm] = pr.sb(f"{tag}_{nm}", [128, w])
    for nm in ("q0", "q1", "q2", "q3", "q4", "q5"):
        z[nm] = pr.sb(f"{tag}_{nm}", [128, nb])
    return z


def emit_quant(ve, gp, scc, z, src_ap, out_ap, w, kind, red=None, src_tok=None):
    base = 0.125 if kind == "act" else 0.5
    clip = 7.5 if kind == "act" else 6.0
    nb = w // 32
    has_stt = ve.e is ve.e.bass.vector
    z0, z1, z2, z3, z4, z5 = (z["z0"], z["z1"], z["z2"], z["z3"], z["z4"], z["z5"])
    q0, q1, q2, q3, q4, q5 = (z["q0"], z["q1"], z["q2"], z["q3"], z["q4"], z["q5"])
    ve.wait(ve.tok())

    def gpow(dst_q, src_q):
        if gp is not ve:
            gp.wait(ve.tok())
        tk = gp.mark(gp.e.tensor_tensor(dst_q, gp.two[:, :nb], src_q, op=ALU.pow))
        ve.wait((gp.s, tk))

    def f2i2f(dst_ap, src_ap_, int_scratch_ap):
        ve.step(ve.e.tensor_copy(int_scratch_ap, src_ap_))
        ve.step(ve.e.tensor_copy(dst_ap, int_scratch_ap))

    if red is None or red.e is ve.e:
        red = ve
    scc.wait(scc.tok())
    scc.wait(ve.tok())
    if src_tok is not None:
        scc.wait(src_tok)
    ta = scc.step(scc.e.activation(z1[:, :], src_ap, AF.Abs))
    red.wait((scc.s, ta))
    if red is not ve:
        red.wait(ve.tok())
    a3 = z1[:, :].rearrange("p (b k) -> p b k", k=32)
    s3 = z2[:, :].rearrange("p (b k) -> p b k", k=32)
    red.step(red.e.tensor_tensor(s3[:, :, 0:16], a3[:, :, 0:16], a3[:, :, 16:32], op=ALU.max))
    for wd in (16, 8, 4):
        red.step(red.e.tensor_tensor(s3[:, :, 0:wd // 2], s3[:, :, 0:wd // 2],
                                     s3[:, :, wd // 2:wd], op=ALU.max))
    red.step(red.e.tensor_tensor(q0[:, :].unsqueeze(2), s3[:, :, 0:1], s3[:, :, 1:2],
                                 op=ALU.max))
    tr_ = red.mark(red.e.tensor_scalar(q0[:, :], q0[:, :], 1e-18, None, op0=ALU.max))
    ve.wait((red.s, tr_))
    scc.wait((red.s, tr_))
    tl = scc.step(scc.e.activation(q1[:, :], q0[:, :], AF.Ln))
    ve.wait((scc.s, tl))
    ve.step(ve.e.tensor_scalar(q1[:, :], q1[:, :], LN2_INV, None, op0=ALU.mult))
    ve.step(ve.e.tensor_scalar(q1[:, :], q1[:, :], 30.0, -32.0, op0=ALU.min, op1=ALU.max))
    ve.step(ve.e.tensor_scalar(q1[:, :], q1[:, :], -0.5, None, op0=ALU.add))
    f2i2f(q2[:, :], q1[:, :], q4[:, :].bitcast(mybir.dt.int32))
    for _ in range(2):
        gpow(q3[:, :], q2[:, :])
        ve.step(ve.e.tensor_tensor(q4[:, :], q3[:, :], q0[:, :], op=ALU.subtract))
        ve.step(ve.e.tensor_scalar(q4[:, :], q4[:, :], 0.0, None, op0=ALU.is_gt))
        ve.step(ve.e.tensor_tensor(q2[:, :], q2[:, :], q4[:, :], op=ALU.subtract))
        ve.step(ve.e.tensor_scalar(q4[:, :], q3[:, :], 2.0, None, op0=ALU.mult))
        ve.step(ve.e.tensor_tensor(q4[:, :], q4[:, :], q0[:, :], op=ALU.subtract))
        ve.step(ve.e.tensor_scalar(q4[:, :], q4[:, :], 0.0, None, op0=ALU.is_le))
        ve.step(ve.e.tensor_tensor(q2[:, :], q2[:, :], q4[:, :], op=ALU.add))
    ve.step(ve.e.tensor_scalar(q4[:, :], q2[:, :], -1.0, 2.0, op0=ALU.mult, op1=ALU.add))
    ve.step(ve.e.tensor_scalar(q5[:, :], q2[:, :], -2.0, None, op0=ALU.add))
    gpow(q3[:, :], q4[:, :])
    gpow(q5[:, :], q5[:, :])
    rsb = q3[:, :].unsqueeze(2).broadcast_to([128, nb, 32])
    upb = q5[:, :].unsqueeze(2).broadcast_to([128, nb, 32])

    src3 = src_ap.rearrange("p (b k) -> p b k", k=32)
    z03 = z0[:, :].rearrange("p (b k) -> p b k", k=32)
    z13 = z1[:, :].rearrange("p (b k) -> p b k", k=32)
    ve.step(ve.e.tensor_tensor(z03, src3, rsb, op=ALU.mult))
    ve.step(ve.e.tensor_tensor(z13, a3, rsb, op=ALU.mult))
    ve.step(ve.e.tensor_scalar(z2[:, :], z1[:, :], 2.0, None, op0=ALU.is_lt))
    ve.step(ve.e.tensor_scalar(z3[:, :], z1[:, :], 4.0, None, op0=ALU.is_lt))
    ve.step(ve.e.tensor_scalar(z4[:, :], z2[:, :], 0.5 / base, 0.25 / base,
                               op0=ALU.mult, op1=ALU.add))
    if has_stt:
        ve.step(ve.e.scalar_tensor_tensor(z4[:, :], z3[:, :], 0.25 / base, z4[:, :],
                                          op0=ALU.mult, op1=ALU.add))
    else:
        ve.step(ve.e.tensor_scalar(z1[:, :], z3[:, :], 0.25 / base, None, op0=ALU.mult))
        ve.step(ve.e.tensor_tensor(z4[:, :], z4[:, :], z1[:, :], op=ALU.add))
    ve.step(ve.e.tensor_scalar(z5[:, :], z2[:, :], -base, 4.0 * base, op0=ALU.mult, op1=ALU.add))
    if has_stt:
        ve.step(ve.e.scalar_tensor_tensor(z5[:, :], z3[:, :], -2.0 * base, z5[:, :],
                                          op0=ALU.mult, op1=ALU.add))
    else:
        ve.step(ve.e.tensor_scalar(z1[:, :], z3[:, :], -2.0 * base, None, op0=ALU.mult))
        ve.step(ve.e.tensor_tensor(z5[:, :], z5[:, :], z1[:, :], op=ALU.add))
    ve.step(ve.e.tensor_tensor(z0[:, :], z0[:, :], z4[:, :], op=ALU.mult))
    ve.step(ve.e.tensor_scalar(z4[:, :], z0[:, :], 0.5, None, op0=ALU.mult))
    f2i2f(z4[:, :], z4[:, :], z3[:, :].bitcast(mybir.dt.int32))
    if has_stt:
        ve.step(ve.e.scalar_tensor_tensor(z1[:, :], z4[:, :], -2.0, z0[:, :],
                                          op0=ALU.mult, op1=ALU.add))
    else:
        ve.step(ve.e.tensor_scalar(z1[:, :], z4[:, :], -2.0, None, op0=ALU.mult))
        ve.step(ve.e.tensor_tensor(z1[:, :], z1[:, :], z0[:, :], op=ALU.add))
    ve.step(ve.e.tensor_scalar(z2[:, :], z1[:, :], 0.5, None, op0=ALU.is_gt))
    ve.step(ve.e.tensor_scalar(z3[:, :], z1[:, :], 1.5, None, op0=ALU.is_lt))
    ve.step(ve.e.tensor_tensor(z2[:, :], z2[:, :], z3[:, :], op=ALU.subtract))
    ve.step(ve.e.tensor_scalar(z3[:, :], z1[:, :], -0.5, None, op0=ALU.is_lt))
    if has_stt:
        ve.step(ve.e.scalar_tensor_tensor(z3[:, :], z1[:, :], -1.5, z3[:, :],
                                          op0=ALU.is_le, op1=ALU.add))
    else:
        ve.step(ve.e.tensor_scalar(z0[:, :], z1[:, :], -1.5, None, op0=ALU.is_le))
        ve.step(ve.e.tensor_tensor(z3[:, :], z3[:, :], z0[:, :], op=ALU.add))
    ve.step(ve.e.tensor_tensor(z2[:, :], z2[:, :], z3[:, :], op=ALU.subtract))
    ve.step(ve.e.tensor_scalar(z0[:, :], z4[:, :], 2.0, 1.0, op0=ALU.mult, op1=ALU.add))
    ve.step(ve.e.tensor_tensor(z2[:, :], z2[:, :], z0[:, :], op=ALU.add))
    ve.step(ve.e.tensor_tensor(z2[:, :], z2[:, :], z5[:, :], op=ALU.mult))
    ve.step(ve.e.tensor_scalar(z2[:, :], z2[:, :], clip, -clip, op0=ALU.min, op1=ALU.max))
    z23 = z2[:, :].rearrange("p (b k) -> p b k", k=32)
    out3 = out_ap.rearrange("p (b k) -> p b k", k=32)
    ve.mark(ve.e.tensor_tensor(out3, z23, upb, op=ALU.mult))
    return ve.tok()



def build_l0(rows_gu, rows_d, hid, i_ffn):
    nc = bass.Bass("TRN2", target_bir_lowering=False, debug=False)
    ins, outs = {}, {}
    for nm, r, w in (("wg", rows_gu, hid), ("wu", rows_gu, hid), ("wd", rows_d, i_ffn)):
        ins[nm] = nc.dram_tensor(nm, [r, w], F32, kind="ExternalInput").ap()
        outs[nm] = nc.dram_tensor(nm + "q", [r, w], BF16, kind="ExternalOutput").ap()
    CW = 2048
    es = contextlib.ExitStack()
    with es:
        pr = Prog(nc, es)
        xin = [pr.sb(f"xin{i}", [128, CW]) for i in range(4)]
        xout = [pr.sb(f"xout{i}", [128, CW], BF16) for i in range(4)]
        zz = [quant_scratch(pr, "zv", CW), quant_scratch(pr, "zg", CW)]

        tiles = []
        for nm, r, w in (("wg", rows_gu, hid), ("wu", rows_gu, hid), ("wd", rows_d, i_ffn)):
            for r0 in range(0, r, 128):
                for c0 in range(0, w, CW):
                    tiles.append((nm, r0, c0))

        block = es.enter_context(nc.Block())

        @block.sync
        def _(sync):
            pr.make_two(CW // 32)
            engs = [pr.V, pr.G]
            sccs = [pr.SC, pr.SC]
            per_eng_count = [0, 0]
            done_tok = {}
            store_tok = {}
            for i, (nm, r0, c0) in enumerate(tiles):
                ei = i % 2
                j = per_eng_count[ei]
                per_eng_count[ei] += 1
                buf = 2 * ei + (j % 2)
                ve, scc = engs[ei], sccs[ei]
                if j >= 2:
                    s, v = done_tok[(ei, j - 2)]
                    nc.sync.wait_ge(s.h, v)
                ldt = pr.load(xin[buf][:, :], ins[nm][r0:r0 + 128, c0:c0 + CW], "sy")
                ve.wait(ldt)
                if j >= 2:
                    ve.wait(store_tok[(ei, j - 2)])
                done_tok[(ei, j)] = emit_quant(ve, pr.G, scc, zz[ei],
                                               xin[buf][:, :], xout[buf][:, :], CW, "w",
                                               red=pr.V, src_tok=ldt)
                s, v = done_tok[(ei, j)]
                nc.scalar.wait_ge(s.h, v)
                store_tok[(ei, j)] = pr.load(outs[nm][r0:r0 + 128, c0:c0 + CW],
                                             xout[buf][:, :], "sc")
            for ei in range(2):
                for j in range(max(0, per_eng_count[ei] - 2), per_eng_count[ei]):
                    s, v = store_tok[(ei, j)]
                    nc.sync.wait_ge(s.h, v)

    return nc



def emit_ln_stats(pr, x_ap, w, lns, eps=1e-5):
    V, SC = pr.V, pr.SC
    k = (w + 511) // 512
    for c in range(k):
        V.s.inc(V.e.bn_stats(lns["bn6"][:, 6 * c:6 * (c + 1)],
                             x_ap[:, 512 * c:min(w, 512 * (c + 1))]))
    V.e.wait_ge(V.s.h, V.s.v)
    V.step(V.e.bn_aggr(lns["mv"][:, :], lns["bn6"][:, :6 * k]))
    V.step(V.e.tensor_scalar(lns["r0"][:, :], lns["mv"][:, 1:2], eps, None, op0=ALU.add))
    SC.wait(V.tok())
    ts = SC.step(SC.e.activation(lns["r1"][:, :], lns["r0"][:, :], AF.Sqrt))
    V.wait((SC.s, ts))
    return V.mark(V.e.reciprocal(lns["r0"][:, :], lns["r1"][:, :]))


def emit_rms_rstd(pr, x_ap, w, lns, junk_ap, eps=1e-6):
    V, SC = pr.V, pr.SC
    SC.wait(V.tok())
    tq = SC.step(SC.e.activation(junk_ap, x_ap, AF.Square))
    V.wait((SC.s, tq))
    V.step(V.e.tensor_reduce(lns["r0"][:, :], junk_ap, axis=AX.X, op=ALU.add))
    V.step(V.e.tensor_scalar(lns["r0"][:, :], lns["r0"][:, :], 1.0 / w, eps,
                             op0=ALU.mult, op1=ALU.add))
    SC.wait(V.tok())
    ts = SC.step(SC.e.activation(lns["r1"][:, :], lns["r0"][:, :], AF.Sqrt))
    V.wait((SC.s, ts))
    return V.mark(V.e.reciprocal(lns["r0"][:, :], lns["r1"][:, :]))


def _copy(ec, dst, src):
    if ec.e is ec.e.bass.scalar:
        return ec.e.copy(dst, src)
    return ec.e.tensor_copy(dst, src)


def emit_transpose(pr, srcs, ptr, dsts, evac=None):
    T = pr.T
    ev = evac or pr.SC
    for g0 in range(0, len(srcs), 4):
        gn = min(4, len(srcs) - g0)
        T.wait(ev.tok())
        for j in range(gn):
            T.s.inc(T.e.transpose(ptr[:, 128 * j:128 * (j + 1)],
                                  srcs[g0 + j], pr.ident[:, :]))
        ev.wait(T.tok())
        for j in range(gn):
            ev.s.inc(_copy(ev, dsts[g0 + j], ptr[:, 128 * j:128 * (j + 1)]))
    return ev.tok()



def emit_phase_lnq(pr, env, xready_toks, quantize):
    V, G, SC, T = pr.V, pr.G, pr.SC, pr.T
    EN, zz, interq, lns = env["EN"], env["zz"], env["interq"], env["lns"]
    x_sb, xfm, ptr = env["x_sb"], env["xfm"], env["ptr"]
    HID, TC, TT, IT = env["HID"], env["TC"], env["TT"], env["IT"]
    NCHK = HID // IT
    tr_tok = {}
    for tt in range(TT):
        if xready_toks is not None:
            V.wait(xready_toks[tt])
        ln_tok = emit_ln_stats(pr, x_sb[:, tt * HID:(tt + 1) * HID], HID, lns)
        for c in range(NCHK):
            tag = "V" if c % 2 == 0 else "G"
            ec = EN[tag]
            z = zz[tag]
            ec.wait((V.s, ln_tok))
            if (tt, c - 2) in tr_tok:
                ec.wait(tr_tok[(tt, c - 2)])
            elif tt > 0 and (tt - 1, c + NCHK - 2) in tr_tok:
                ec.wait(tr_tok[(tt - 1, c + NCHK - 2)])
            if quantize:
                ec.step(ec.e.tensor_scalar(z["z0"][:, :],
                                           x_sb[:, tt * HID + c * IT: tt * HID + (c + 1) * IT],
                                           lns["mv"][:, 0:1], lns["r0"][:, :],
                                           op0=ALU.subtract, op1=ALU.mult))
                qt = emit_quant(ec, pr.G, SC, z, z["z0"][:, :], interq[tag][:, :],
                                IT, "act", red=pr.V, src_tok=ec.tok())
            else:
                ec.step(ec.e.tensor_scalar(interq[tag][:, :],
                                           x_sb[:, tt * HID + c * IT: tt * HID + (c + 1) * IT],
                                           lns["mv"][:, 0:1], lns["r0"][:, :],
                                           op0=ALU.subtract, op1=ALU.mult))
                qt = ec.tok()
            T.wait(qt)
            if tt == 0 and c == 0 and env.get("xfm_war") is not None:
                T.wait(env["xfm_war"])
            srcs = [interq[tag][:, j * 128:(j + 1) * 128] for j in range(IT // 128)]
            fc0 = (c * IT) // 128
            dsts = [xfm[:, (fc0 + j) * TC + tt * 128: (fc0 + j) * TC + (tt + 1) * 128]
                    for j in range(IT // 128)]
            tr_tok[(tt, c)] = emit_transpose(pr, srcs, ptr, dsts)
        V.wait(G.tok())
    env["xfm_war"] = None


def emit_phase_ffn(pr, env, sync, load_w, wtok):
    V, G, SC, T = pr.V, pr.G, pr.SC, pr.T
    EN, zz, interq, ifm = env["EN"], env["zz"], env["interq"], env["ifm"]
    x_sb, xfm, ptr = env["x_sb"], env["xfm"], env["ptr"]
    pg, pu, pd = env["pg"], env["pu"], env["pd"]
    wg_sb, wu_sb, wd_sb = env["wg_sb"], env["wu_sb"], env["wd_sb"]
    HID, TC, TT, IT, NI, FC = (env["HID"], env["TC"], env["TT"], env["IT"],
                               env["NI"], env["FC"])
    T.wait((SC.s, SC.s.v))
    gu_tok, q_tok, ifm_tok, dn_tok, xev_tok = {}, {}, {}, {}, {}
    silu_tok = {}
    for it in range(NI):
        bb = it % 2
        if it + 1 < NI:
            if it >= 1:
                sync.wait_ge(T.s.h, dn_tok[(it - 1, TT - 1)][1])
            load_w(it + 1)
        for t in wtok[it]:
            T.wait(t)
        for tt in range(TT + 1):
            if tt < TT:
                tag = "V" if tt % 2 == 0 else "G"
                ec = EN[tag]
                z = zz[tag]
                prev = (it, tt - 1) if tt >= 1 else (it - 1, TT - 1)
                if prev in silu_tok:
                    T.wait(silu_tok[prev])
                    T.wait(q_tok[prev][2])
                for fc in range(FC):
                    T.s.inc(T.e.matmul(
                        pg[:, :], xfm[:, fc * TC + tt * 128:fc * TC + (tt + 1) * 128],
                        wg_sb[bb][:, fc * IT:(fc + 1) * IT],
                        start=(fc == 0), stop=(fc == FC - 1)))
                for fc in range(FC):
                    T.s.inc(T.e.matmul(
                        pu[:, :], xfm[:, fc * TC + tt * 128:fc * TC + (tt + 1) * 128],
                        wu_sb[bb][:, fc * IT:(fc + 1) * IT],
                        start=(fc == 0), stop=(fc == FC - 1)))
                gu_tok[(it, tt)] = T.tok()
                SC.wait(gu_tok[(it, tt)])
                silu_tok[(it, tt)] = (SC.s, SC.mark(
                    SC.e.activation(z["z0"][:, :], pg[:, :], AF.Silu)))
                mul_ec = ec if tag == "V" else pr.V
                mul_ec.wait(gu_tok[(it, tt)])
                mul_ec.wait(silu_tok[(it, tt)])
                if (it, tt - 2) in ifm_tok:
                    mul_ec.wait(ifm_tok[(it, tt - 2)])
                elif it >= 1 and (it - 1, tt + TT - 2) in ifm_tok:
                    mul_ec.wait(ifm_tok[(it - 1, tt + TT - 2)])
                if tag != "V":
                    mul_ec.wait(ec.tok())
                tm = mul_ec.mark(mul_ec.e.tensor_tensor(z["z0"][:, :], pu[:, :],
                                                        z["z0"][:, :], op=ALU.mult))
                ec.wait((mul_ec.s, tm))
                up_done = (mul_ec.s, tm)
                qt = emit_quant(ec, pr.G, SC, z, z["z0"][:, :], interq[tag][:, :],
                                IT, "act", red=pr.V, src_tok=up_done)
                q_tok[(it, tt)] = (qt[0], qt[1], up_done)
            pt = tt - 1
            if pt >= 0:
                tag = "V" if pt % 2 == 0 else "G"
                T.wait(q_tok[(it, pt)][:2])
                srcs = [interq[tag][:, ic * 128:(ic + 1) * 128]
                        for ic in range(IT // 128)]
                dsts = [ifm[tag][:, ic * 128:(ic + 1) * 128]
                        for ic in range(IT // 128)]
                ifm_tok[(it, pt)] = emit_transpose(pr, srcs, ptr, dsts)
                T.wait(ifm_tok[(it, pt)])
                if (it, pt - 1) in xev_tok:
                    T.wait(xev_tok[(it, pt - 1)])
                elif it >= 1 and (it - 1, TT - 1) in xev_tok:
                    T.wait(xev_tok[(it - 1, TT - 1)])
                nic = IT // 128
                for ic in range(nic):
                    for ot in range(HID // 512):
                        T.s.inc(T.e.matmul(
                            pd[:, ot * 512:(ot + 1) * 512],
                            ifm[tag][:, ic * 128:(ic + 1) * 128],
                            wd_sb[bb][:, ic * HID + ot * 512: ic * HID + (ot + 1) * 512],
                            start=(ic == 0), stop=(ic == nic - 1)))
                dn_tok[(it, pt)] = T.tok()
                V.wait(dn_tok[(it, pt)])
                for ot in range(HID // 512):
                    V.s.inc(V.e.scalar_tensor_tensor(
                        x_sb[:, pt * HID + ot * 512: pt * HID + (ot + 1) * 512],
                        pd[:, ot * 512:(ot + 1) * 512], 0.5,
                        x_sb[:, pt * HID + ot * 512: pt * HID + (ot + 1) * 512],
                        op0=ALU.mult, op1=ALU.add))
                xev_tok[(it, pt)] = V.tok()
    return gu_tok, dn_tok, xev_tok



def build_l1(cfg, TC):
    HID, I, QLR, KVLR, DR, H = (cfg["HID"], cfg["I"], cfg["QLR"], cfg["KVLR"],
                                cfg["DR"], cfg["H"])
    DN = cfg["DN"]
    DQ = DN + DR
    HDQ = H * DQ
    TT, FC, QC = TC // 128, HID // 128, QLR // 128
    IT = 256
    NI = I // IT
    KD = KVLR + DR
    NQP = 4
    PARTC = HDQ // NQP
    NH = H // NQP

    nc = bass.Bass("TRN2", target_bir_lowering=False, debug=False)
    x_h = nc.dram_tensor("x", [TC, HID], F32, kind="ExternalInput").ap()
    wgq_h = nc.dram_tensor("wgq", [NI, 128, FC * IT], BF16, kind="ExternalInput").ap()
    wuq_h = nc.dram_tensor("wuq", [NI, 128, FC * IT], BF16, kind="ExternalInput").ap()
    wdq_h = nc.dram_tensor("wdq", [NI, 128, (IT // 128) * HID], BF16, kind="ExternalInput").ap()
    wqa_h = nc.dram_tensor("wqa", [2, 128, (FC // 2) * QLR], F32, kind="ExternalInput").ap()
    wkva_h = nc.dram_tensor("wkva", [128, FC * KD], F32, kind="ExternalInput").ap()
    wqb_h = nc.dram_tensor("wqb", [NQP, 128, QC * PARTC], F32, kind="ExternalInput").ap()
    qgc_h = nc.dram_tensor("qgc", [128, QC], F32, kind="ExternalInput").ap()
    x2_h = nc.dram_tensor("x2", [TC, HID], F32, kind="ExternalOutput").ap()
    qr_h = nc.dram_tensor("qr", [TC, HDQ], BF16, kind="ExternalOutput").ap()
    ckvl_h = nc.dram_tensor("ckvl", [TC, KVLR], BF16, kind="ExternalOutput").ap()
    kpe_h = nc.dram_tensor("kpe", [TC, DR], BF16, kind="ExternalOutput").ap()

    AR_F32 = (2 * FC * IT + (IT // 128) * HID) // 2
    es = contextlib.ExitStack()
    with es:
        pr = Prog(nc, es)
        x_sb = pr.sb("x_sb", [128, TT * HID])
        xfm = pr.sb("xfm", [128, FC * TC], BF16)
        waren = [pr.sb(f"waren{i}", [128, AR_F32]) for i in range(2)]
        lnout = pr.sb("lnout", [128, max(QLR, KD)])
        zz = {t: quant_scratch(pr, "z" + t, IT) for t in ("V", "G")}
        interq = {t: pr.sb(f"interq{t}", [128, IT], BF16) for t in ("V", "G")}
        ifm = {t: pr.sb(f"ifm{t}", [128, IT], BF16) for t in ("V", "G")}
        cos_t = pr.sb("cos_t", [128, TT * 64])
        sin_t = pr.sb("sin_t", [128, TT * 64])
        trig0 = pr.sb("trig0", [128, max(TT * 64, NH * 64)])
        trig1 = pr.sb("trig1", [128, max(2 * TT * 64, 64 + NH * 64)])
        offvec = pr.sb("offvec", [128, 1])
        lns = {"bn6": pr.sb("bn6", [128, 6 * ((HID + 511) // 512)]),
               "mv": pr.sb("mv", [128, 2]), "r0": pr.sb("r0", [128, 1]),
               "r1": pr.sb("r1", [128, 1])}
        cqn = pr.sb("cqn", [128, QLR], BF16)
        ckvl_st = pr.sb("ckvl_st", [128, KVLR], BF16)
        kpe_st = pr.sb("kpe_st", [128, DR], BF16)
        qr_st = pr.sb("qr_st", [128, PARTC], BF16)
        cqfm = [pr.sb(f"cqfm{qi}", [128, TC], BF16) for qi in range(QC)]
        pg = pr.ps("pg", [128, IT])
        pu = pr.ps("pu", [128, IT])
        pd = pr.ps("pd", [128, HID])
        pq = pr.ps("pq", [128, 512])
        ptr = es.enter_context(nc.psum_tensor("ptr", [128, 512], BF16))

        V, G, SC, T, SY = pr.V, pr.G, pr.SC, pr.T, pr.SY
        EN = {"V": V, "G": G}
        block = es.enter_context(nc.Block())

        @block.sync
        def _(sync):
            itok = pr.make_ident()
            T.wait(itok)
            pr.make_two(IT // 32)
            for arm in nc.gpsimd.switch_core_id(n=128):
                nc.gpsimd.memset(offvec[:, :], float((arm.logical % NCORES) * TC))
            invf = sin_t[:, 0:64]
            iscr = sin_t[:, 64:128]
            G.step(G.e.iota(invf, [[1, 64]], base=0, channel_multiplier=0,
                            allow_small_or_imprecise_dtypes=True))
            V.wait(G.tok())
            V.step(V.e.tensor_scalar(iscr, invf, 31.5, None, op0=ALU.is_gt))
            V.step(V.e.scalar_tensor_tensor(invf, iscr, -32.0, invf,
                                            op0=ALU.mult, op1=ALU.add))
            SC.wait(V.tok())
            th = SC.step(SC.e.activation(invf, invf, AF.Exp,
                                         scale=-(2.0 / DR) * float(np.log(10000.0))))
            G.wait((SC.s, th))
            G.step(G.e.iota(trig0[:, :TT * 64], [[128, TT], [0, 64]], base=0,
                            channel_multiplier=1, allow_small_or_imprecise_dtypes=True))
            G.step(G.e.tensor_scalar(trig0[:, :TT * 64], trig0[:, :TT * 64],
                                     offvec[:, :], None, op0=ALU.add))
            invf3 = invf.unsqueeze(1).broadcast_to([128, TT, 64])
            tr3 = trig0[:, :TT * 64].rearrange("p (t j) -> p t j", j=64)
            G.step(G.e.tensor_tensor(tr3, tr3, invf3, op=ALU.mult))
            C1, C2 = 6.28125, float(2 * np.pi - 6.28125)
            W64 = TT * 64
            u_ap = trig1[:, 0:W64]
            acc_ap = trig1[:, W64:2 * W64]
            for dst, shift in ((sin_t, 0.5), (cos_t, 0.75)):
                G.step(G.e.tensor_scalar(u_ap, trig0[:, :W64],
                                         float(1 / (2 * np.pi)), shift,
                                         op0=ALU.mult, op1=ALU.add))
                V.wait(G.tok())
                V.step(V.e.tensor_scalar(acc_ap, u_ap, 0.0, None, op0=ALU.mult))
                kk = 1024.0
                while kk >= 1.0:
                    V.step(V.e.scalar_tensor_tensor(dst[:, :W64], acc_ap, kk, u_ap,
                                                    op0=ALU.add, op1=ALU.is_le))
                    V.step(V.e.scalar_tensor_tensor(acc_ap, dst[:, :W64], kk, acc_ap,
                                                    op0=ALU.mult, op1=ALU.add))
                    kk /= 2.0
                V.step(V.e.scalar_tensor_tensor(dst[:, :W64], acc_ap, -C1,
                                                trig0[:, :W64], op0=ALU.mult, op1=ALU.add))
                V.step(V.e.scalar_tensor_tensor(dst[:, :W64], acc_ap, -C2,
                                                dst[:, :W64], op0=ALU.mult, op1=ALU.add))
                if shift == 0.75:
                    V.step(V.e.tensor_scalar(dst[:, :W64], dst[:, :W64],
                                             float(np.pi / 2), None, op0=ALU.add))
                SC.wait(V.tok())
                ts_ = SC.step(SC.e.activation(dst[:, :W64], dst[:, :W64], AF.Sin))
                V.wait((SC.s, ts_))
                G.wait((SC.s, ts_))
            trig_tok = (SC.s, SC.s.v)

            xtoks = [pr.load(x_sb[:, tt * HID:(tt + 1) * HID],
                             x_h[tt * 128:(tt + 1) * 128, :], "sy") for tt in range(TT)]

            wg_sb, wu_sb, wd_sb = {}, {}, {}
            for bb in range(2):
                ap = waren[bb].ap().bitcast(BF16)
                wg_sb[bb] = ap[:, 0:FC * IT]
                wu_sb[bb] = ap[:, FC * IT:2 * FC * IT]
                wd_sb[bb] = ap[:, 2 * FC * IT:2 * FC * IT + (IT // 128) * HID]
            wtok = {}

            def load_w(it):
                bb = it % 2
                wtok[it] = (pr.load(wg_sb[bb], wgq_h[it, :, :], "sy"),
                            pr.load(wu_sb[bb], wuq_h[it, :, :], "sy"),
                            pr.load(wd_sb[bb], wdq_h[it, :, :], "sy"))

            load_w(0)

            env = dict(EN=EN, zz=zz, interq=interq, ifm=ifm, lns=lns,
                       x_sb=x_sb[:, :], xfm=xfm[:, :], ptr=ptr,
                       pg=pg, pu=pu, pd=pd, wg_sb=wg_sb, wu_sb=wu_sb, wd_sb=wd_sb,
                       HID=HID, TC=TC, TT=TT, IT=IT, NI=NI, FC=FC, xfm_war=None)
            emit_phase_lnq(pr, env, xtoks, quantize=True)
            gu_tok, dn_tok, xev_tok = emit_phase_ffn(pr, env, sync, load_w, wtok)

            x2_store = []
            for tt in range(TT):
                SY.wait(xev_tok[(NI - 1, tt)])
                x2_store.append(pr.load(x2_h[tt * 128:(tt + 1) * 128, :],
                                        x_sb[:, tt * HID:(tt + 1) * HID], "sy"))

            for tt in range(TT):
                V.wait(xev_tok[(NI - 1, tt)])
            env["xfm_war"] = gu_tok[(NI - 1, TT - 1)]
            emit_phase_lnq(pr, env, None, quantize=False)
            trc_last = (SC.s, SC.s.v)

            nc.gpsimd.wait_ge(T.s.h, dn_tok[(NI - 1, TT - 1)][1])
            wqa_sb = [waren[0].ap().bitcast(BF16)[:, :(FC // 2) * QLR],
                      waren[1].ap().bitcast(BF16)[:, :(FC // 2) * QLR]]
            qa_tok = [pr.load(wqa_sb[hh], wqa_h[hh, :, :], "g") for hh in range(2)]
            for t in x2_store:
                V.wait(t)
            junk = x_sb
            pq_ev = None
            cq_tr = {}
            for tt in range(TT):
                T.wait(trc_last)
                for t in qa_tok:
                    T.wait(t)
                for qt in range(QLR // 512):
                    if pq_ev is not None:
                        T.wait(pq_ev)
                    for fc in range(FC):
                        rhs = wqa_sb[fc // (FC // 2)][:, (fc % (FC // 2)) * QLR + qt * 512:
                                                      (fc % (FC // 2)) * QLR + (qt + 1) * 512]
                        T.s.inc(T.e.matmul(
                            pq[:, :], xfm[:, fc * TC + tt * 128:fc * TC + (tt + 1) * 128],
                            rhs, start=(fc == 0), stop=(fc == FC - 1)))
                    V.wait(T.tok())
                    V.s.inc(V.e.tensor_copy(lnout[:, qt * 512:(qt + 1) * 512], pq[:, :]))
                    pq_ev = V.tok()
                V.e.wait_ge(V.s.h, V.s.v)
                rt = emit_rms_rstd(pr, lnout[:, :QLR], QLR, lns, junk[:, :QLR])
                V.wait((V.s, rt))
                V.step(V.e.tensor_scalar(cqn[:, :], lnout[:, :QLR], lns["r0"][:, :],
                                         None, op0=ALU.mult))
                T.wait(V.tok())
                srcs = [cqn[:, qc * 128:(qc + 1) * 128] for qc in range(QC)]
                dsts = [cqfm[qc][:, tt * 128:(tt + 1) * 128] for qc in range(QC)]
                cq_tr[tt] = emit_transpose(pr, srcs, ptr, dsts, evac=V)

            nc.gpsimd.wait_ge(T.s.h, T.s.v)
            wkva_sb = waren[0].ap().bitcast(BF16)[:, :FC * KD]
            kva_tok = pr.load(wkva_sb, wkva_h[:, :], "g")
            ck_store = kp_store = None
            for tt in range(TT):
                T.wait(kva_tok)
                c0 = 0
                while c0 < KD:
                    cw = min(512, KD - c0)
                    if pq_ev is not None:
                        T.wait(pq_ev)
                    for fc in range(FC):
                        T.s.inc(T.e.matmul(
                            pq[:, :cw], xfm[:, fc * TC + tt * 128:fc * TC + (tt + 1) * 128],
                            wkva_sb[:, fc * KD + c0: fc * KD + c0 + cw],
                            start=(fc == 0), stop=(fc == FC - 1)))
                    V.wait(T.tok())
                    V.s.inc(V.e.tensor_copy(lnout[:, c0:c0 + cw], pq[:, :cw]))
                    pq_ev = V.tok()
                    c0 += cw
                V.e.wait_ge(V.s.h, V.s.v)
                rt = emit_rms_rstd(pr, lnout[:, :KVLR], KVLR, lns, junk[:, :KVLR])
                V.wait((V.s, rt))
                if ck_store is not None:
                    V.wait(ck_store)
                V.step(V.e.tensor_scalar(ckvl_st[:, :], lnout[:, :KVLR], lns["r0"][:, :],
                                         None, op0=ALU.mult))
                SY.wait(V.tok())
                ck_store = pr.load(ckvl_h[tt * 128:(tt + 1) * 128, :], ckvl_st[:, :], "sy")
                pe = lnout[:, KVLR:KD]
                cs = cos_t[:, tt * 64:(tt + 1) * 64]
                sn = sin_t[:, tt * 64:(tt + 1) * 64]
                V.step(V.e.tensor_tensor(trig1[:, 0:64], pe, cs, op=ALU.mult))
                V.step(V.e.tensor_tensor(trig1[:, 64:96], pe[:, 32:64], sn[:, 0:32], op=ALU.mult))
                V.step(V.e.tensor_tensor(trig1[:, 96:128], pe[:, 0:32], sn[:, 32:64], op=ALU.mult))
                if kp_store is not None:
                    V.wait(kp_store)
                V.step(V.e.tensor_tensor(kpe_st[:, 0:32], trig1[:, 0:32], trig1[:, 64:96],
                                         op=ALU.subtract))
                V.step(V.e.tensor_tensor(kpe_st[:, 32:64], trig1[:, 32:64], trig1[:, 96:128],
                                         op=ALU.add))
                SY.wait(V.tok())
                kp_store = pr.load(kpe_h[tt * 128:(tt + 1) * 128, :], kpe_st[:, :], "sy")

            nc.sync.wait_ge(V.s.h, V.s.v)
            gq_tok = pr.load(trig1[:, 0:QC], qgc_h[:, :], "sy")
            qr_store = None
            for pp in range(NQP):
                nc.gpsimd.wait_ge(T.s.h, T.s.v)
                nc.gpsimd.wait_ge(V.s.h, V.s.v)
                wqb_sb = waren[0].ap().bitcast(BF16)[:, :QC * PARTC]
                ptile = waren[1].ap()[:, :PARTC]
                qb_tok = pr.load(wqb_sb, wqb_h[pp, :, :], "g")
                G.wait(qb_tok)
                G.wait(gq_tok)
                for qc in range(QC):
                    G.step(G.e.tensor_scalar(wqb_sb[:, qc * PARTC:(qc + 1) * PARTC],
                                             wqb_sb[:, qc * PARTC:(qc + 1) * PARTC],
                                             trig1[:, qc:qc + 1], None, op0=ALU.mult))
                wqb_ready = G.tok()
                T.wait(wqb_ready)
                for tt in range(TT):
                    T.wait(cq_tr[tt])
                    c0 = 0
                    while c0 < PARTC:
                        cw = min(512, PARTC - c0)
                        if pq_ev is not None:
                            T.wait(pq_ev)
                        for qc in range(QC):
                            T.s.inc(T.e.matmul(
                                pq[:, :cw], cqfm[qc][:, tt * 128:(tt + 1) * 128],
                                wqb_sb[:, qc * PARTC + c0: qc * PARTC + c0 + cw],
                                start=(qc == 0), stop=(qc == QC - 1)))
                        V.wait(T.tok())
                        V.s.inc(V.e.tensor_copy(ptile[:, c0:c0 + cw], pq[:, :cw]))
                        pq_ev = V.tok()
                        c0 += cw
                    V.e.wait_ge(V.s.h, V.s.v)
                    pt3 = ptile[:, :NH * DQ].rearrange("p (h d) -> p h d", d=DQ)
                    qr3 = qr_st[:, :].rearrange("p (h d) -> p h d", d=DQ)
                    cs3 = cos_t[:, tt * 64:(tt + 1) * 64].unsqueeze(1).broadcast_to([128, NH, 64])
                    sn3 = sin_t[:, tt * 64:(tt + 1) * 64].unsqueeze(1).broadcast_to([128, NH, 64])
                    t13 = trig0[:, :NH * 64].rearrange("p (h d) -> p h d", d=64)
                    r3 = trig1[:, 64:64 + NH * 64].rearrange("p (h d) -> p h d", d=64)
                    pe3 = pt3[:, :, DN:DQ]
                    if qr_store is not None:
                        V.wait(qr_store)
                    V.step(V.e.tensor_copy(qr3[:, :, 0:DN], pt3[:, :, 0:DN]))
                    V.step(V.e.tensor_tensor(t13, pe3, cs3, op=ALU.mult))
                    V.step(V.e.tensor_tensor(r3[:, :, 0:32], pe3[:, :, 32:64],
                                             sn3[:, :, 0:32], op=ALU.mult))
                    V.step(V.e.tensor_tensor(r3[:, :, 32:64], pe3[:, :, 0:32],
                                             sn3[:, :, 32:64], op=ALU.mult))
                    V.step(V.e.tensor_tensor(qr3[:, :, DN:DN + 32], t13[:, :, 0:32],
                                             r3[:, :, 0:32], op=ALU.subtract))
                    V.step(V.e.tensor_tensor(qr3[:, :, DN + 32:DQ], t13[:, :, 32:64],
                                             r3[:, :, 32:64], op=ALU.add))
                    SY.wait(V.tok())
                    qr_store = pr.load(qr_h[tt * 128:(tt + 1) * 128, pp * PARTC:(pp + 1) * PARTC],
                                       qr_st[:, :], "sy")

            for q in ("sy", "sc", "g"):
                for s in pr.dma_slots[q]:
                    if s.v > 0:
                        sync.wait_ge(s.h, s.v)
            for ecx in (V, G, SC, T):
                sync.wait_ge(ecx.s.h, ecx.s.v)

    return nc



def build_l2(cfg, HPC):
    S, KVLR, DR, DN, DV = cfg["S"], cfg["KVLR"], cfg["DR"], cfg["DN"], cfg["DV"]
    DQ = DN + DR
    KC = KVLR // 128
    ST = S // 128
    NKV = DN + DV
    SCALE = float(DQ ** -0.5)
    CLAMP_OFF = 87.0 / SCALE

    nc = bass.Bass("TRN2", target_bir_lowering=False, debug=False)
    q4_h = nc.dram_tensor("q4", [HPC, 128, ST * DQ], BF16, kind="ExternalInput").ap()
    ckvl_h = nc.dram_tensor("ckvl", [S, KVLR], BF16, kind="ExternalInput").ap()
    kpe_h = nc.dram_tensor("kpe", [S, DR], BF16, kind="ExternalInput").ap()
    wkvb_h = nc.dram_tensor("wkvb", [128, KC * HPC * NKV], F32, kind="ExternalInput").ap()
    kgc_h = nc.dram_tensor("kgc", [128, KC], F32, kind="ExternalInput").ap()
    o4_h = nc.dram_tensor("o4", [HPC * DV, S], BF16, kind="ExternalOutput").ap()

    es = contextlib.ExitStack()
    with es:
        pr = Prog(nc, es)
        ckv_fm = pr.sb("ckv_fm", [128, KC * S], BF16)
        kpe_fm = pr.sb("kpe_fm", [128, S], BF16)
        stg = [pr.sb(f"stg{i}", [128, max(KVLR, DQ)], BF16) for i in range(2)]
        qst = [pr.sb(f"qst{i}", [128, ST * DQ], BF16) for i in range(2)]
        q_fm = pr.sb("q_fm", [128, S], BF16)
        qpe_fm = pr.sb("qpe_fm", [128, S], BF16)
        k_fm = pr.sb("k_fm", [128, S], BF16)
        v_fm = pr.sb("v_fm", [128, S], BF16)
        v_tm = pr.sb("v_tm", [128, ST * DV], BF16)
        wkvb_sb = pr.sb("wkvb_sb", [128, KC * HPC * NKV], BF16)
        srow = pr.sb("srow", [128, S])
        probs = [pr.sb(f"probs{i}", [128, S], BF16) for i in range(4)]
        pts = pr.sb("pts", [128, 512], BF16)
        o_sb = pr.sb("o_sb", [128, S], BF16)
        msk = pr.sb("msk", [128, 128])
        nm = pr.sb("nm", [128, 1])
        lo = pr.sb("lo", [128, 1])
        sums = pr.sb("sums", [128, 1])
        rs = pr.sb("rs", [128, 1])
        psc = pr.ps("psc", [128, 512])
        po = pr.ps("po", [128, 512])
        pkv = pr.ps("pkv", [128, 512])
        ptr = es.enter_context(nc.psum_tensor("ptr", [128, 512], BF16))

        V, G, SC, T, SY = pr.V, pr.G, pr.SC, pr.T, pr.SY
        block = es.enter_context(nc.Block())

        @block.sync
        def _(sync):
            itok = pr.make_ident()
            T.wait(itok)
            G.step(G.e.memset(msk[:, :], 0.0))
            G.step(G.e.affine_select(msk[:, :], msk[:, :], [[-1, 128]], ALU.is_ge,
                                     NEG_BIG, base=0, channel_multiplier=1))
            mask_tok = G.tok()
            V.wait(mask_tok)

            wk_tok = pr.load(wkvb_sb[:, :], wkvb_h[:, :], "g")
            kgc_sb = srow
            kg_tok = pr.load(kgc_sb[:, 0:KC], kgc_h[:, :], "sy")
            G.wait(wk_tok)
            G.wait(kg_tok)
            for kc in range(KC):
                G.step(G.e.tensor_scalar(wkvb_sb[:, kc * HPC * NKV:(kc + 1) * HPC * NKV],
                                         wkvb_sb[:, kc * HPC * NKV:(kc + 1) * HPC * NKV],
                                         kgc_sb[:, kc:kc + 1], None, op0=ALU.mult))
            wkvb_ready = G.tok()
            T.wait(wkvb_ready)

            ld_prev = {}
            for st in range(ST):
                b = st % 2
                if (st - 2) in ld_prev:
                    pass
                ct = pr.load(stg[b][:, :KVLR], ckvl_h[st * 128:(st + 1) * 128, :], "sy")
                T.wait(ct)
                srcs = [stg[b][:, kc * 128:(kc + 1) * 128] for kc in range(KC)]
                dsts = [ckv_fm[:, kc * S + st * 128: kc * S + (st + 1) * 128] for kc in range(KC)]
                tr = emit_transpose(pr, srcs, ptr, dsts)
                SY.wait(tr)
                ld_prev[st] = tr
            for st in range(ST):
                b = st % 2
                kt = pr.load(stg[b][:, :DR], kpe_h[st * 128:(st + 1) * 128, :], "sy")
                T.wait(kt)
                T.wait(SC.tok())
                T.s.inc(T.e.transpose(ptr[0:DR, 0:128], stg[b][:, :DR], pr.ident[:, :]))
                SC.wait(T.tok())
                SC.s.inc(SC.e.copy(kpe_fm[0:DR, st * 128:(st + 1) * 128], ptr[0:DR, 0:128]))
                SY.wait(SC.tok())
            kpe_ready = SC.tok()

            qld = {0: pr.load(qst[0][:, :], q4_h[0, :, :], "sy")}
            o_store = None
            for h in range(HPC):
                if h + 1 < HPC:
                    qld[h + 1] = pr.load(qst[(h + 1) % 2][:, :], q4_h[h + 1, :, :], "sy")
                qb = qst[h % 2]
                T.wait(qld[h])
                for st in range(ST):
                    T.wait(SC.tok())
                    T.s.inc(T.e.transpose(ptr[:, 0:128], qb[:, st * DQ: st * DQ + DN],
                                          pr.ident[:, :]))
                    T.s.inc(T.e.transpose(ptr[0:DR, 128:256], qb[:, st * DQ + DN: (st + 1) * DQ],
                                          pr.ident[:, :]))
                    SC.wait(T.tok())
                    SC.s.inc(SC.e.copy(q_fm[:, st * 128:(st + 1) * 128], ptr[:, 0:128]))
                    SC.s.inc(SC.e.copy(qpe_fm[0:DR, st * 128:(st + 1) * 128], ptr[0:DR, 128:256]))
                q_ready = SC.tok()
                T.wait(q_ready)
                T.wait(kpe_ready)

                kv_ev = None
                for part, dst in ((0, k_fm), (1, v_fm)):
                    for sc0 in range(S // 512):
                        if kv_ev is not None:
                            T.wait(kv_ev)
                        for kc in range(KC):
                            lhs = wkvb_sb[:, kc * HPC * NKV + h * NKV + part * DN:
                                          kc * HPC * NKV + h * NKV + part * DN + 128]
                            T.s.inc(T.e.matmul(pkv[:, :], lhs,
                                               ckv_fm[:, kc * S + sc0 * 512:kc * S + (sc0 + 1) * 512],
                                               start=(kc == 0), stop=(kc == KC - 1)))
                        SC.wait(T.tok())
                        SC.s.inc(SC.e.copy(dst[:, sc0 * 512:(sc0 + 1) * 512], pkv[:, :]))
                        kv_ev = SC.tok()
                T.wait(kv_ev)
                for st in range(ST):
                    T.wait(SC.tok())
                    T.s.inc(T.e.transpose(ptr[:, 0:128], v_fm[:, st * 128:(st + 1) * 128],
                                          pr.ident[:, :]))
                    SC.wait(T.tok())
                    SC.s.inc(SC.e.copy(v_tm[:, st * 128:(st + 1) * 128], ptr[:, 0:128]))
                v_ready = SC.tok()

                grp_tok = None
                sc_ev = []
                srow_free = None
                for qg in range(ST // 4):
                    ktiles = qg + 1
                    pb_tok = []
                    for qi in range(4):
                        qt = qg * 4 + qi
                        kn = (qt + 1) * 128
                        nk5 = (kn + 511) // 512
                        for kt in range(nk5):
                            cw = min(512, kn - kt * 512)
                            for t in sc_ev:
                                T.wait(t)
                            T.s.inc(T.e.matmul(psc[:, :cw], q_fm[:, qt * 128:(qt + 1) * 128],
                                               k_fm[:, kt * 512:kt * 512 + cw],
                                               start=True, stop=False))
                            T.s.inc(T.e.matmul(psc[:, :cw], qpe_fm[0:DR, qt * 128:(qt + 1) * 128],
                                               kpe_fm[0:DR, kt * 512:kt * 512 + cw],
                                               start=False, stop=True))
                            smm = T.tok()
                            d0 = qt * 128 - kt * 512
                            sc_ev = []
                            if 0 <= d0 < cw:
                                if d0 > 0:
                                    SC.wait(smm)
                                    if srow_free is not None:
                                        SC.wait(srow_free)
                                    SC.s.inc(SC.e.copy(srow[:, kt * 512:kt * 512 + d0],
                                                       psc[:, :d0]))
                                    sc_ev.append((SC.s, SC.s.v))
                                V.wait(smm)
                                V.s.inc(V.e.tensor_tensor(srow[:, qt * 128:qt * 128 + 128],
                                                          psc[:, d0:d0 + 128], msk[:, :],
                                                          op=ALU.add))
                                sc_ev.append((V.s, V.s.v))
                            else:
                                SC.wait(smm)
                                if srow_free is not None:
                                    SC.wait(srow_free)
                                SC.s.inc(SC.e.copy(srow[:, kt * 512:kt * 512 + cw], psc[:, :cw]))
                                sc_ev.append((SC.s, SC.s.v))
                        for t in sc_ev:
                            V.wait(t)
                        V.wait((SC.s, SC.s.v))
                        V.e.wait_ge(V.s.h, V.s.v)
                        V.step(V.e.tensor_reduce(nm[:, :], srow[:, :kn], axis=AX.X,
                                                 op=ALU.max, negate=True))
                        V.step(V.e.tensor_scalar(lo[:, :], nm[:, :], -1.0, -CLAMP_OFF,
                                                 op0=ALU.mult, op1=ALU.add))
                        V.step(V.e.tensor_scalar(nm[:, :], nm[:, :], SCALE, None, op0=ALU.mult))
                        V.step(V.e.tensor_scalar(srow[:, :kn], srow[:, :kn], lo[:, :],
                                                 None, op0=ALU.max))
                        SC.wait(V.tok())
                        et = SC.step(SC.e.activation(srow[:, :kn], srow[:, :kn], AF.Exp,
                                                     bias=nm[:, :], scale=SCALE,
                                                     accum_out=sums[:, :]))
                        V.wait((SC.s, et))
                        V.step(V.e.reciprocal(rs[:, :], sums[:, :]))
                        if grp_tok is not None:
                            V.wait(grp_tok)
                        V.step(V.e.tensor_scalar(probs[qi][:, :kn], srow[:, :kn],
                                                 rs[:, :], None, op0=ALU.mult))
                        srow_free = V.tok()
                        if kn < ktiles * 512:
                            V.step(V.e.memset(probs[qi][:, kn:ktiles * 512], 0.0))
                        pb_tok.append(V.tok())
                    for t in pb_tok:
                        T.wait(t)
                    T.wait(v_ready)
                    for kt in range(ktiles * 4):
                        T.wait(SC.tok())
                        for qi in range(4):
                            T.s.inc(T.e.transpose(ptr[:, qi * 128:(qi + 1) * 128],
                                                  probs[qi][:, kt * 128:(kt + 1) * 128],
                                                  pr.ident[:, :]))
                        SC.wait(T.tok())
                        SC.s.inc(SC.e.copy(pts[:, :], ptr[:, :]))
                        T.wait(SC.tok())
                        T.s.inc(T.e.matmul(po[:, :], v_tm[:, kt * 128:(kt + 1) * 128],
                                           pts[:, :], start=(kt == 0),
                                           stop=(kt == ktiles * 4 - 1)))
                    grp_tok = T.tok()
                    SC.wait(grp_tok)
                    if o_store is not None and qg == 0:
                        SC.wait(o_store)
                    SC.s.inc(SC.e.copy(o_sb[:, qg * 512:(qg + 1) * 512], po[:, :]))
                SY.wait(SC.tok())
                o_store = pr.load(o4_h[h * DV:(h + 1) * DV, :], o_sb[:, :], "sy")

            for q in ("sy", "sc", "g"):
                for s_ in pr.dma_slots[q]:
                    if s_.v > 0:
                        sync.wait_ge(s_.h, s_.v)
            for ecx in (V, G, SC, T):
                sync.wait_ge(ecx.s.h, ecx.s.v)

    return nc




def build_l3(cfg, TC):
    HID, I, H, DV = cfg["HID"], cfg["I"], cfg["H"], cfg["DV"]
    TT, FC = TC // 128, HID // 128
    IT = 256
    NI = I // IT
    OC = (H * DV) // 128
    NWQ = 4
    OCQ = OC // NWQ

    nc = bass.Bass("TRN2", target_bir_lowering=False, debug=False)
    x2_h = nc.dram_tensor("x2", [TC, HID], F32, kind="ExternalInput").ap()
    ofm_h = nc.dram_tensor("ofm", [128, OC * TC], BF16, kind="ExternalInput").ap()
    wo_h = nc.dram_tensor("wo", [NWQ, 128, OCQ * HID], F32, kind="ExternalInput").ap()
    wgq_h = nc.dram_tensor("wgq", [NI, 128, FC * IT], BF16, kind="ExternalInput").ap()
    wuq_h = nc.dram_tensor("wuq", [NI, 128, FC * IT], BF16, kind="ExternalInput").ap()
    wdq_h = nc.dram_tensor("wdq", [NI, 128, (IT // 128) * HID], BF16, kind="ExternalInput").ap()
    out_h = nc.dram_tensor("out", [TC, HID], F32, kind="ExternalOutput").ap()

    AR_F32 = (2 * FC * IT + (IT // 128) * HID) // 2
    assert OCQ * HID * 2 <= AR_F32 * 4, "wo quarter must fit an arena"
    es = contextlib.ExitStack()
    with es:
        pr = Prog(nc, es)
        x_sb = pr.sb("x_sb", [128, TT * HID])
        xfm = pr.sb("xfm", [128, max(FC, OC) * TC], BF16)
        waren = [pr.sb(f"waren{i}", [128, AR_F32]) for i in range(2)]
        zz = {t: quant_scratch(pr, "z" + t, IT) for t in ("V", "G")}
        interq = {t: pr.sb(f"interq{t}", [128, IT], BF16) for t in ("V", "G")}
        ifm = {t: pr.sb(f"ifm{t}", [128, IT], BF16) for t in ("V", "G")}
        lns = {"bn6": pr.sb("bn6", [128, 6 * ((HID + 511) // 512)]),
               "mv": pr.sb("mv", [128, 2]), "r0": pr.sb("r0", [128, 1]),
               "r1": pr.sb("r1", [128, 1])}
        pg = pr.ps("pg", [128, IT])
        pu = pr.ps("pu", [128, IT])
        pd = pr.ps("pd", [128, HID])
        ptr = es.enter_context(nc.psum_tensor("ptr", [128, 512], BF16))

        V, G, SC, T, SY = pr.V, pr.G, pr.SC, pr.T, pr.SY
        EN = {"V": V, "G": G}
        block = es.enter_context(nc.Block())

        @block.sync
        def _(sync):
            itok = pr.make_ident()
            T.wait(itok)
            pr.make_two(IT // 32)
            xtoks = [pr.load(x_sb[:, tt * HID:(tt + 1) * HID],
                             x2_h[tt * 128:(tt + 1) * 128, :], "sy") for tt in range(TT)]
            of_tok = pr.load(xfm[:, 0:OC * TC], ofm_h[:, :], "sy")

            wo_sb = [waren[b].ap().bitcast(BF16)[:, :OCQ * HID] for b in range(2)]
            wq_tok = {0: pr.load(wo_sb[0], wo_h[0, :, :], "g")}
            xev = {}
            for qq in range(NWQ):
                bb = qq % 2
                if qq + 1 < NWQ:
                    if qq >= 1:
                        nc.gpsimd.wait_ge(T.s.h, T.s.v)
                    wq_tok[qq + 1] = pr.load(wo_sb[(qq + 1) % 2], wo_h[qq + 1, :, :], "g")
                T.wait(wq_tok[qq])
                T.wait(of_tok)
                for tt in range(TT):
                    if (qq, tt - 1) in xev:
                        T.wait(xev[(qq, tt - 1)])
                    elif qq >= 1 and (qq - 1, TT - 1) in xev:
                        T.wait(xev[(qq - 1, TT - 1)])
                    for oc4 in range(OCQ):
                        oc = qq * OCQ + oc4
                        for ot in range(HID // 512):
                            T.s.inc(T.e.matmul(
                                pd[:, ot * 512:(ot + 1) * 512],
                                xfm[:, oc * TC + tt * 128:oc * TC + (tt + 1) * 128],
                                wo_sb[bb][:, oc4 * HID + ot * 512: oc4 * HID + (ot + 1) * 512],
                                start=(oc4 == 0), stop=(oc4 == OCQ - 1)))
                    wt = T.tok()
                    V.wait(wt)
                    V.wait(xtoks[tt])
                    for ot in range(HID // 512):
                        V.s.inc(V.e.scalar_tensor_tensor(
                            x_sb[:, tt * HID + ot * 512: tt * HID + (ot + 1) * 512],
                            pd[:, ot * 512:(ot + 1) * 512], 1.0,
                            x_sb[:, tt * HID + ot * 512: tt * HID + (ot + 1) * 512],
                            op0=ALU.mult, op1=ALU.add))
                    xev[(qq, tt)] = V.tok()
            wo_done = [xev[(NWQ - 1, tt)] for tt in range(TT)]

            wg_sb, wu_sb, wd_sb = {}, {}, {}
            for bb in range(2):
                ap = waren[bb].ap().bitcast(BF16)
                wg_sb[bb] = ap[:, 0:FC * IT]
                wu_sb[bb] = ap[:, FC * IT:2 * FC * IT]
                wd_sb[bb] = ap[:, 2 * FC * IT:2 * FC * IT + (IT // 128) * HID]
            wtok = {}

            def load_w(it):
                bb = it % 2
                wtok[it] = (pr.load(wg_sb[bb], wgq_h[it, :, :], "sy"),
                            pr.load(wu_sb[bb], wuq_h[it, :, :], "sy"),
                            pr.load(wd_sb[bb], wdq_h[it, :, :], "sy"))

            sync.wait_ge(T.s.h, T.s.v)
            load_w(0)

            env = dict(EN=EN, zz=zz, interq=interq, ifm=ifm, lns=lns,
                       x_sb=x_sb[:, :], xfm=xfm[:, :], ptr=ptr,
                       pg=pg, pu=pu, pd=pd, wg_sb=wg_sb, wu_sb=wu_sb, wd_sb=wd_sb,
                       HID=HID, TC=TC, TT=TT, IT=IT, NI=NI, FC=FC,
                       xfm_war=(T.s, T.s.v))
            emit_phase_lnq(pr, env, wo_done, quantize=True)
            gu_tok, dn_tok, xev_tok = emit_phase_ffn(pr, env, sync, load_w, wtok)

            for tt in range(TT):
                SY.wait(xev_tok[(NI - 1, tt)])
                pr.load(out_h[tt * 128:(tt + 1) * 128, :],
                        x_sb[:, tt * HID:(tt + 1) * HID], "sy")

            for q in ("sy", "sc", "g"):
                for s_ in pr.dma_slots[q]:
                    if s_.v > 0:
                        sync.wait_ge(s_.h, s_.v)
            for ecx in (V, G, SC, T):
                sync.wait_ge(ecx.s.h, ecx.s.v)

    return nc


def _split_rows(a, n):
    return [np.ascontiguousarray(x) for x in np.split(a, n, axis=0)]


def _layout_ffn_w(wq_T, NI, IT, FC):
    HID = wq_T.shape[0]
    return np.ascontiguousarray(
        wq_T.reshape(FC, 128, NI, IT).transpose(2, 1, 0, 3).reshape(NI, 128, FC * IT))


def kernel(L0_attn_hidden_states, L0_attn_w_qa, L0_attn_q_a_ln_w, L0_attn_w_qb,
           L0_attn_w_kva, L0_attn_kv_a_ln_w, L0_attn_w_kvb, L0_attn_w_o,
           L0_ffn_w_gate, L0_ffn_w_up, L0_ffn_w_down):
    cfg = CFG
    B, S, HID, I = cfg["B"], cfg["S"], cfg["HID"], cfg["I"]
    QLR, KVLR, DR, H, DN, DV = (cfg["QLR"], cfg["KVLR"], cfg["DR"], cfg["H"],
                                cfg["DN"], cfg["DV"])
    DQ = DN + DR
    HDQ = H * DQ
    TC = B * S // NCORES
    TT, FC, QC, KC = TC // 128, HID // 128, QLR // 128, KVLR // 128
    IT = 256
    NI = I // IT
    KD = KVLR + DR
    NQP = 4
    PARTC = HDQ // NQP
    HPC = H // (NCORES // B)
    ST = S // 128
    OC = H * DV // 128
    NWQ = 4
    OCQ = OC // NWQ
    cores = list(range(NCORES))

    wg = np.ascontiguousarray(np.asarray(L0_ffn_w_gate, np.float32))
    wu = np.ascontiguousarray(np.asarray(L0_ffn_w_up, np.float32))
    wd = np.ascontiguousarray(np.asarray(L0_ffn_w_down, np.float32))
    nc0 = build_l0(wg.shape[0] // NCORES, wd.shape[0] // NCORES, HID, I)
    in0 = [{"wg": a, "wu": b, "wd": c}
           for a, b, c in zip(_split_rows(wg, NCORES), _split_rows(wu, NCORES),
                              _split_rows(wd, NCORES))]
    r0 = run_bass_kernel_spmd(nc0, in0, core_ids=cores)
    wgq = np.concatenate([r["wgq"] for r in r0.results], axis=0)
    wuq = np.concatenate([r["wuq"] for r in r0.results], axis=0)
    wdq = np.concatenate([r["wdq"] for r in r0.results], axis=0)

    l_wg = _layout_ffn_w(np.ascontiguousarray(wgq.T), NI, IT, FC)
    l_wu = _layout_ffn_w(np.ascontiguousarray(wuq.T), NI, IT, FC)
    wdqT = np.ascontiguousarray(wdq.T)
    l_wd = np.ascontiguousarray(
        wdqT.reshape(NI, IT // 128, 128, HID).transpose(0, 2, 1, 3)
        .reshape(NI, 128, (IT // 128) * HID))
    wqaT = np.ascontiguousarray(np.asarray(L0_attn_w_qa, np.float32).T)
    l_wqa = np.ascontiguousarray(
        wqaT.reshape(2, FC // 2, 128, QLR).transpose(0, 2, 1, 3)
        .reshape(2, 128, (FC // 2) * QLR))
    wkvaT = np.ascontiguousarray(np.asarray(L0_attn_w_kva, np.float32).T)
    l_wkva = np.ascontiguousarray(
        wkvaT.reshape(FC, 128, KD).transpose(1, 0, 2).reshape(128, FC * KD))
    wqbT = np.ascontiguousarray(np.asarray(L0_attn_w_qb, np.float32).T)
    l_wqb = np.ascontiguousarray(
        wqbT.reshape(QC, 128, NQP, PARTC).transpose(2, 1, 0, 3)
        .reshape(NQP, 128, QC * PARTC))
    l_qgc = np.ascontiguousarray(
        np.asarray(L0_attn_q_a_ln_w, np.float32).reshape(QC, 128).T)
    hs = np.ascontiguousarray(np.asarray(L0_attn_hidden_states, np.float32)
                              .reshape(B * S, HID))
    xs = _split_rows(hs, NCORES)
    nc1 = build_l1(cfg, TC)
    in1 = [{"x": xs[c], "wgq": l_wg, "wuq": l_wu, "wdq": l_wd, "wqa": l_wqa,
            "wkva": l_wkva, "wqb": l_wqb, "qgc": l_qgc} for c in cores]
    r1 = run_bass_kernel_spmd(nc1, in1, core_ids=cores)
    x2 = [r1.results[c]["x2"] for c in cores]
    qr = [r1.results[c]["qr"] for c in cores]
    ckvl = [r1.results[c]["ckvl"] for c in cores]
    kpe = [r1.results[c]["kpe"] for c in cores]

    G4 = NCORES // B
    wkvbT = np.ascontiguousarray(np.asarray(L0_attn_w_kvb, np.float32).T)
    NKV = DN + DV
    l_kgc = np.ascontiguousarray(
        np.asarray(L0_attn_kv_a_ln_w, np.float32).reshape(KC, 128).T)
    in2 = []
    for c in cores:
        b, slot = c // G4, c % G4
        ck_full = np.concatenate([ckvl[b * G4 + j] for j in range(G4)], axis=0)
        kp_full = np.concatenate([kpe[b * G4 + j] for j in range(G4)], axis=0)
        qr_full = np.concatenate([qr[b * G4 + j] for j in range(G4)], axis=0)
        q4 = np.empty((HPC, 128, ST * DQ), dtype=qr_full.dtype)
        for hh in range(HPC):
            gh = slot * HPC + hh
            q4[hh] = np.ascontiguousarray(
                qr_full[:, gh * DQ:(gh + 1) * DQ].reshape(ST, 128, DQ)
                .transpose(1, 0, 2).reshape(128, ST * DQ))
        wsl = wkvbT[:, slot * HPC * NKV:(slot + 1) * HPC * NKV]
        l_wkvb = np.ascontiguousarray(
            wsl.reshape(KC, 128, HPC * NKV).transpose(1, 0, 2)
            .reshape(128, KC * HPC * NKV))
        in2.append({"q4": q4, "ckvl": np.ascontiguousarray(ck_full),
                    "kpe": np.ascontiguousarray(kp_full), "wkvb": l_wkvb,
                    "kgc": l_kgc})
    nc2 = build_l2(cfg, HPC)
    r2 = run_bass_kernel_spmd(nc2, in2, core_ids=cores)
    o4 = [r2.results[c]["o4"] for c in cores]

    woT = np.ascontiguousarray(np.asarray(L0_attn_w_o, np.float32).T)
    l_wo = np.ascontiguousarray(
        woT.reshape(NWQ, OCQ, 128, HID).transpose(0, 2, 1, 3)
        .reshape(NWQ, 128, OCQ * HID))
    in3 = []
    for c in cores:
        b, slot = c // G4, c % G4
        O_fm = np.concatenate([o4[b * G4 + j] for j in range(G4)], axis=0)
        ofm = O_fm[:, slot * TC:(slot + 1) * TC]
        l_ofm = np.ascontiguousarray(
            ofm.reshape(OC, 128, TC).transpose(1, 0, 2).reshape(128, OC * TC))
        in3.append({"x2": x2[c], "ofm": l_ofm, "wo": l_wo, "wgq": l_wg,
                    "wuq": l_wu, "wdq": l_wd})
    nc3 = build_l3(cfg, TC)
    r3 = run_bass_kernel_spmd(nc3, in3, core_ids=cores)
    out = np.concatenate([r3.results[c]["out"] for c in cores], axis=0)
    return np.ascontiguousarray(out.reshape(B, S, HID)).astype(np.float32)


# version 08
# speedup vs baseline: 1.0003x; 1.0003x over previous
import contextlib
import numpy as np
import ml_dtypes
import concourse.bass as bass
import concourse.mybir as mybir
from concourse.bass_utils import run_bass_kernel_spmd

F32 = mybir.dt.float32
BF16 = mybir.dt.bfloat16
AF = mybir.ActivationFunctionType
ALU = mybir.AluOpType
AX = mybir.AxisListType
BF16NP = ml_dtypes.bfloat16

LN2_INV = float(np.float32(1.4426950408889634))
NEG_BIG = -1.0e30
NCORES = 8

CFG = dict(B=2, S=4096, HID=2048, I=8192, QLR=1536, KVLR=512, H=16,
           DN=128, DR=64, DV=128)


class Sem:

    def __init__(self, handle):
        self.h = handle
        self.v = 0

    def inc(self, instr, n=1):
        instr.then_inc(self.h, n)
        self.v += n
        return self.v


class EC:

    def __init__(self, eng, sem):
        self.e = eng
        self.s = sem

    def step(self, instr):
        self.s.inc(instr)
        self.e.wait_ge(self.s.h, self.s.v)
        return self.s.v

    def mark(self, instr, n=1):
        return self.s.inc(instr, n)

    def tok(self):
        return (self.s, self.s.v)

    def wait(self, tok):
        sem, v = tok
        if v > 0:
            self.e.wait_ge(sem.h, v)


class Prog:

    def __init__(self, nc, es):
        self.nc = nc
        self.es = es
        self.V = EC(nc.vector, self.sem("semV"))
        self.G = EC(nc.gpsimd, self.sem("semG"))
        self.SC = EC(nc.scalar, self.sem("semSC"))
        self.T = EC(nc.tensor, self.sem("semT"))
        self.SY = EC(nc.sync, self.sem("semSY"))
        self.dma_slots = {
            q: [Sem(self.es.enter_context(nc.semaphore(f"dma_{q}{i}"))) for i in range(4)]
            for q in ("sy", "sc", "g")
        }
        self.dma_idx = {q: 0 for q in ("sy", "sc", "g")}

    def sem(self, name):
        return Sem(self.es.enter_context(self.nc.semaphore(name)))

    def sb(self, name, shape, dtype=F32):
        return self.es.enter_context(self.nc.sbuf_tensor(name, shape, dtype))

    def ps(self, name, shape, dtype=F32):
        return self.es.enter_context(self.nc.psum_tensor(name, shape, dtype))

    def make_two(self, maxnb):
        self.G.two = self.sb("g_two", [128, maxnb])
        self.G.step(self.G.e.memset(self.G.two[:, :], 2.0))

    def make_ident(self):
        self.ident = self.sb("ident_t", [128, 128], BF16)
        g = self.G
        g.step(g.e.memset(self.ident[:, :], 1.0))
        g.step(g.e.affine_select(self.ident[:, :], self.ident[:, :], [[1, 128]],
                                 ALU.is_equal, 0.0, base=0, channel_multiplier=-1))
        return g.tok()

    def load(self, dst_ap, src_ap, queue="sy"):
        eng = {"sy": self.nc.sync, "sc": self.nc.scalar, "g": self.nc.gpsimd}[queue]
        i = self.dma_idx[queue]
        self.dma_idx[queue] = (i + 1) % 4
        sem = self.dma_slots[queue][i]
        if sem.v > 0:
            eng.wait_ge(sem.h, sem.v)
        sem.inc(eng.dma_start(out=dst_ap, in_=src_ap), 16)
        return (sem, sem.v)


def quant_scratch(pr, tag, w):
    nb = w // 32
    z = {"w": w}
    for nm in ("z0", "z1", "z2", "z3", "z4", "z5"):
        z[nm] = pr.sb(f"{tag}_{nm}", [128, w])
    for nm in ("q0", "q1", "q2", "q3", "q4", "q5"):
        z[nm] = pr.sb(f"{tag}_{nm}", [128, nb])
    z["zi"] = pr.sb(f"{tag}_zi", [128, w], mybir.dt.int32)
    z["qi"] = pr.sb(f"{tag}_qi", [128, nb], mybir.dt.int32)
    return z


def emit_quant(ve, gp, scc, z, src_ap, out_ap, w, kind, red=None, src_tok=None):
    base = 0.125 if kind == "act" else 0.5
    clip = 7.5 if kind == "act" else 6.0
    nb = w // 32
    has_stt = ve.e is ve.e.bass.vector
    z0, z1, z2, z3, z4, z5 = (z["z0"], z["z1"], z["z2"], z["z3"], z["z4"], z["z5"])
    q0, q1, q2, q3, q4, q5 = (z["q0"], z["q1"], z["q2"], z["q3"], z["q4"], z["q5"])
    ve.wait(ve.tok())

    def gpow(dst_q, src_q):
        if gp is not ve:
            gp.wait(ve.tok())
        tk = gp.mark(gp.e.tensor_tensor(dst_q, gp.two[:, :nb], src_q, op=ALU.pow))
        ve.wait((gp.s, tk))

    def f2i2f(dst_ap, src_ap_, int_scratch_ap):
        ve.step(ve.e.tensor_copy(int_scratch_ap, src_ap_))
        ve.step(ve.e.tensor_copy(dst_ap, int_scratch_ap))

    if red is None or red.e is ve.e:
        red = ve
    scc.wait(scc.tok())
    scc.wait(ve.tok())
    if src_tok is not None:
        scc.wait(src_tok)
    ta = scc.step(scc.e.activation(z1[:, :], src_ap, AF.Abs))
    red.wait((scc.s, ta))
    if red is not ve:
        red.wait(ve.tok())
    a3 = z1[:, :].rearrange("p (b k) -> p b k", k=32)
    s3 = z2[:, :].rearrange("p (b k) -> p b k", k=32)
    red.step(red.e.tensor_tensor(s3[:, :, 0:16], a3[:, :, 0:16], a3[:, :, 16:32], op=ALU.max))
    for wd in (16, 8, 4):
        red.step(red.e.tensor_tensor(s3[:, :, 0:wd // 2], s3[:, :, 0:wd // 2],
                                     s3[:, :, wd // 2:wd], op=ALU.max))
    red.step(red.e.tensor_tensor(q0[:, :].unsqueeze(2), s3[:, :, 0:1], s3[:, :, 1:2],
                                 op=ALU.max))
    tr_ = red.mark(red.e.tensor_scalar(q0[:, :], q0[:, :], 1e-18, None, op0=ALU.max))
    ve.wait((red.s, tr_))
    scc.wait((red.s, tr_))
    tl = scc.step(scc.e.activation(q1[:, :], q0[:, :], AF.Ln))
    ve.wait((scc.s, tl))
    ve.step(ve.e.tensor_scalar(q1[:, :], q1[:, :], LN2_INV, None, op0=ALU.mult))
    ve.step(ve.e.tensor_scalar(q1[:, :], q1[:, :], 30.0, -32.0, op0=ALU.min, op1=ALU.max))
    ve.step(ve.e.tensor_scalar(q1[:, :], q1[:, :], -0.5, None, op0=ALU.add))
    f2i2f(q2[:, :], q1[:, :], z["qi"][:, :])
    for _ in range(2):
        gpow(q3[:, :], q2[:, :])
        ve.step(ve.e.tensor_tensor(q4[:, :], q3[:, :], q0[:, :], op=ALU.subtract))
        ve.step(ve.e.tensor_scalar(q4[:, :], q4[:, :], 0.0, None, op0=ALU.is_gt))
        ve.step(ve.e.tensor_tensor(q2[:, :], q2[:, :], q4[:, :], op=ALU.subtract))
        ve.step(ve.e.tensor_scalar(q4[:, :], q3[:, :], 2.0, None, op0=ALU.mult))
        ve.step(ve.e.tensor_tensor(q4[:, :], q4[:, :], q0[:, :], op=ALU.subtract))
        ve.step(ve.e.tensor_scalar(q4[:, :], q4[:, :], 0.0, None, op0=ALU.is_le))
        ve.step(ve.e.tensor_tensor(q2[:, :], q2[:, :], q4[:, :], op=ALU.add))
    ve.step(ve.e.tensor_scalar(q4[:, :], q2[:, :], -1.0, 2.0, op0=ALU.mult, op1=ALU.add))
    ve.step(ve.e.tensor_scalar(q5[:, :], q2[:, :], -2.0, None, op0=ALU.add))
    gpow(q3[:, :], q4[:, :])
    gpow(q5[:, :], q5[:, :])
    rsb = q3[:, :].unsqueeze(2).broadcast_to([128, nb, 32])
    upb = q5[:, :].unsqueeze(2).broadcast_to([128, nb, 32])

    src3 = src_ap.rearrange("p (b k) -> p b k", k=32)
    z03 = z0[:, :].rearrange("p (b k) -> p b k", k=32)
    z13 = z1[:, :].rearrange("p (b k) -> p b k", k=32)
    ve.step(ve.e.tensor_tensor(z03, src3, rsb, op=ALU.mult))
    ve.step(ve.e.tensor_tensor(z13, a3, rsb, op=ALU.mult))
    ve.step(ve.e.tensor_scalar(z2[:, :], z1[:, :], 2.0, None, op0=ALU.is_lt))
    ve.step(ve.e.tensor_scalar(z3[:, :], z1[:, :], 4.0, None, op0=ALU.is_lt))
    ve.step(ve.e.tensor_scalar(z4[:, :], z2[:, :], 0.5 / base, 0.25 / base,
                               op0=ALU.mult, op1=ALU.add))
    if has_stt:
        ve.step(ve.e.scalar_tensor_tensor(z4[:, :], z3[:, :], 0.25 / base, z4[:, :],
                                          op0=ALU.mult, op1=ALU.add))
    else:
        ve.step(ve.e.tensor_scalar(z1[:, :], z3[:, :], 0.25 / base, None, op0=ALU.mult))
        ve.step(ve.e.tensor_tensor(z4[:, :], z4[:, :], z1[:, :], op=ALU.add))
    ve.step(ve.e.tensor_scalar(z5[:, :], z2[:, :], -base, 4.0 * base, op0=ALU.mult, op1=ALU.add))
    if has_stt:
        ve.step(ve.e.scalar_tensor_tensor(z5[:, :], z3[:, :], -2.0 * base, z5[:, :],
                                          op0=ALU.mult, op1=ALU.add))
    else:
        ve.step(ve.e.tensor_scalar(z1[:, :], z3[:, :], -2.0 * base, None, op0=ALU.mult))
        ve.step(ve.e.tensor_tensor(z5[:, :], z5[:, :], z1[:, :], op=ALU.add))
    ve.step(ve.e.tensor_tensor(z0[:, :], z0[:, :], z4[:, :], op=ALU.mult))
    ve.step(ve.e.tensor_scalar(z4[:, :], z0[:, :], 0.5, None, op0=ALU.mult))
    f2i2f(z4[:, :], z4[:, :], z["zi"][:, :])
    if has_stt:
        ve.step(ve.e.scalar_tensor_tensor(z1[:, :], z4[:, :], -2.0, z0[:, :],
                                          op0=ALU.mult, op1=ALU.add))
    else:
        ve.step(ve.e.tensor_scalar(z1[:, :], z4[:, :], -2.0, None, op0=ALU.mult))
        ve.step(ve.e.tensor_tensor(z1[:, :], z1[:, :], z0[:, :], op=ALU.add))
    ve.step(ve.e.tensor_scalar(z2[:, :], z1[:, :], 0.5, None, op0=ALU.is_gt))
    ve.step(ve.e.tensor_scalar(z3[:, :], z1[:, :], 1.5, None, op0=ALU.is_lt))
    ve.step(ve.e.tensor_tensor(z2[:, :], z2[:, :], z3[:, :], op=ALU.subtract))
    ve.step(ve.e.tensor_scalar(z3[:, :], z1[:, :], -0.5, None, op0=ALU.is_lt))
    if has_stt:
        ve.step(ve.e.scalar_tensor_tensor(z3[:, :], z1[:, :], -1.5, z3[:, :],
                                          op0=ALU.is_le, op1=ALU.add))
    else:
        ve.step(ve.e.tensor_scalar(z0[:, :], z1[:, :], -1.5, None, op0=ALU.is_le))
        ve.step(ve.e.tensor_tensor(z3[:, :], z3[:, :], z0[:, :], op=ALU.add))
    ve.step(ve.e.tensor_tensor(z2[:, :], z2[:, :], z3[:, :], op=ALU.subtract))
    ve.step(ve.e.tensor_scalar(z0[:, :], z4[:, :], 2.0, 1.0, op0=ALU.mult, op1=ALU.add))
    ve.step(ve.e.tensor_tensor(z2[:, :], z2[:, :], z0[:, :], op=ALU.add))
    ve.step(ve.e.tensor_tensor(z2[:, :], z2[:, :], z5[:, :], op=ALU.mult))
    ve.step(ve.e.tensor_scalar(z2[:, :], z2[:, :], clip, -clip, op0=ALU.min, op1=ALU.max))
    z23 = z2[:, :].rearrange("p (b k) -> p b k", k=32)
    out3 = out_ap.rearrange("p (b k) -> p b k", k=32)
    ve.mark(ve.e.tensor_tensor(out3, z23, upb, op=ALU.mult))
    return ve.tok()



def build_l0(rows_gu, rows_d, hid, i_ffn):
    nc = bass.Bass("TRN2", target_bir_lowering=False, debug=False)
    ins, outs = {}, {}
    for nm, r, w in (("wg", rows_gu, hid), ("wu", rows_gu, hid), ("wd", rows_d, i_ffn)):
        ins[nm] = nc.dram_tensor(nm, [r, w], F32, kind="ExternalInput").ap()
        outs[nm] = nc.dram_tensor(nm + "q", [r, w], BF16, kind="ExternalOutput").ap()
    CW = 2048
    es = contextlib.ExitStack()
    with es:
        pr = Prog(nc, es)
        xin = [pr.sb(f"xin{i}", [128, CW]) for i in range(4)]
        xout = [pr.sb(f"xout{i}", [128, CW], BF16) for i in range(4)]
        zz = [quant_scratch(pr, "zv", CW), quant_scratch(pr, "zg", CW)]

        tiles = []
        for nm, r, w in (("wg", rows_gu, hid), ("wu", rows_gu, hid), ("wd", rows_d, i_ffn)):
            for r0 in range(0, r, 128):
                for c0 in range(0, w, CW):
                    tiles.append((nm, r0, c0))

        block = es.enter_context(nc.Block())

        @block.sync
        def _(sync):
            pr.make_two(CW // 32)
            engs = [pr.V, pr.G]
            sccs = [pr.SC, pr.SC]
            per_eng_count = [0, 0]
            done_tok = {}
            store_tok = {}
            for i, (nm, r0, c0) in enumerate(tiles):
                ei = i % 2
                j = per_eng_count[ei]
                per_eng_count[ei] += 1
                buf = 2 * ei + (j % 2)
                ve, scc = engs[ei], sccs[ei]
                if j >= 2:
                    s, v = done_tok[(ei, j - 2)]
                    nc.sync.wait_ge(s.h, v)
                ldt = pr.load(xin[buf][:, :], ins[nm][r0:r0 + 128, c0:c0 + CW], "sy")
                ve.wait(ldt)
                if j >= 2:
                    ve.wait(store_tok[(ei, j - 2)])
                done_tok[(ei, j)] = emit_quant(ve, pr.G, scc, zz[ei],
                                               xin[buf][:, :], xout[buf][:, :], CW, "w",
                                               red=pr.V, src_tok=ldt)
                s, v = done_tok[(ei, j)]
                nc.scalar.wait_ge(s.h, v)
                store_tok[(ei, j)] = pr.load(outs[nm][r0:r0 + 128, c0:c0 + CW],
                                             xout[buf][:, :], "sc")
            for ei in range(2):
                for j in range(max(0, per_eng_count[ei] - 2), per_eng_count[ei]):
                    s, v = store_tok[(ei, j)]
                    nc.sync.wait_ge(s.h, v)

    return nc



def emit_ln_stats(pr, x_ap, w, lns, eps=1e-5):
    V, SC = pr.V, pr.SC
    k = (w + 511) // 512
    for c in range(k):
        V.s.inc(V.e.bn_stats(lns["bn6"][:, 6 * c:6 * (c + 1)],
                             x_ap[:, 512 * c:min(w, 512 * (c + 1))]))
    V.e.wait_ge(V.s.h, V.s.v)
    V.step(V.e.bn_aggr(lns["mv"][:, :], lns["bn6"][:, :6 * k]))
    V.step(V.e.tensor_scalar(lns["r0"][:, :], lns["mv"][:, 1:2], eps, None, op0=ALU.add))
    SC.wait(V.tok())
    ts = SC.step(SC.e.activation(lns["r1"][:, :], lns["r0"][:, :], AF.Sqrt))
    V.wait((SC.s, ts))
    return V.mark(V.e.reciprocal(lns["r0"][:, :], lns["r1"][:, :]))


def emit_rms_rstd(pr, x_ap, w, lns, junk_ap, eps=1e-6):
    V, SC = pr.V, pr.SC
    SC.wait(V.tok())
    tq = SC.step(SC.e.activation(junk_ap, x_ap, AF.Square))
    V.wait((SC.s, tq))
    V.step(V.e.tensor_reduce(lns["r0"][:, :], junk_ap, axis=AX.X, op=ALU.add))
    V.step(V.e.tensor_scalar(lns["r0"][:, :], lns["r0"][:, :], 1.0 / w, eps,
                             op0=ALU.mult, op1=ALU.add))
    SC.wait(V.tok())
    ts = SC.step(SC.e.activation(lns["r1"][:, :], lns["r0"][:, :], AF.Sqrt))
    V.wait((SC.s, ts))
    return V.mark(V.e.reciprocal(lns["r0"][:, :], lns["r1"][:, :]))


def _copy(ec, dst, src):
    if ec.e is ec.e.bass.scalar:
        return ec.e.copy(dst, src)
    return ec.e.tensor_copy(dst, src)


def emit_transpose(pr, srcs, ptr, dsts, evac=None):
    T = pr.T
    ev = evac or pr.SC
    for g0 in range(0, len(srcs), 4):
        gn = min(4, len(srcs) - g0)
        T.wait(ev.tok())
        for j in range(gn):
            T.s.inc(T.e.transpose(ptr[:, 128 * j:128 * (j + 1)],
                                  srcs[g0 + j], pr.ident[:, :]))
        ev.wait(T.tok())
        for j in range(gn):
            ev.s.inc(_copy(ev, dsts[g0 + j], ptr[:, 128 * j:128 * (j + 1)]))
    return ev.tok()



def emit_phase_lnq(pr, env, xready_toks, quantize):
    V, G, SC, T = pr.V, pr.G, pr.SC, pr.T
    EN, zz, interq, lns = env["EN"], env["zz"], env["interq"], env["lns"]
    x_sb, xfm, ptr = env["x_sb"], env["xfm"], env["ptr"]
    HID, TC, TT, IT = env["HID"], env["TC"], env["TT"], env["IT"]
    NCHK = HID // IT
    tr_tok = {}
    for tt in range(TT):
        if xready_toks is not None:
            V.wait(xready_toks[tt])
        ln_tok = emit_ln_stats(pr, x_sb[:, tt * HID:(tt + 1) * HID], HID, lns)
        for c in range(NCHK):
            tag = "V" if c % 2 == 0 else "G"
            ec = EN[tag]
            z = zz[tag]
            ec.wait((V.s, ln_tok))
            if (tt, c - 2) in tr_tok:
                ec.wait(tr_tok[(tt, c - 2)])
            elif tt > 0 and (tt - 1, c + NCHK - 2) in tr_tok:
                ec.wait(tr_tok[(tt - 1, c + NCHK - 2)])
            if quantize:
                ec.step(ec.e.tensor_scalar(z["z0"][:, :],
                                           x_sb[:, tt * HID + c * IT: tt * HID + (c + 1) * IT],
                                           lns["mv"][:, 0:1], lns["r0"][:, :],
                                           op0=ALU.subtract, op1=ALU.mult))
                qt = emit_quant(ec, pr.G, SC, z, z["z0"][:, :], interq[tag][:, :],
                                IT, "act", red=pr.V, src_tok=ec.tok())
            else:
                ec.step(ec.e.tensor_scalar(interq[tag][:, :],
                                           x_sb[:, tt * HID + c * IT: tt * HID + (c + 1) * IT],
                                           lns["mv"][:, 0:1], lns["r0"][:, :],
                                           op0=ALU.subtract, op1=ALU.mult))
                qt = ec.tok()
            T.wait(qt)
            if tt == 0 and c == 0 and env.get("xfm_war") is not None:
                T.wait(env["xfm_war"])
            srcs = [interq[tag][:, j * 128:(j + 1) * 128] for j in range(IT // 128)]
            fc0 = (c * IT) // 128
            dsts = [xfm[:, (fc0 + j) * TC + tt * 128: (fc0 + j) * TC + (tt + 1) * 128]
                    for j in range(IT // 128)]
            tr_tok[(tt, c)] = emit_transpose(pr, srcs, ptr, dsts)
        V.wait(G.tok())
    env["xfm_war"] = None


def emit_phase_ffn(pr, env, sync, load_w, wtok):
    V, G, SC, T = pr.V, pr.G, pr.SC, pr.T
    EN, zz, interq, ifm = env["EN"], env["zz"], env["interq"], env["ifm"]
    x_sb, xfm, ptr = env["x_sb"], env["xfm"], env["ptr"]
    pg, pu, pd = env["pg"], env["pu"], env["pd"]
    wg_sb, wu_sb, wd_sb = env["wg_sb"], env["wu_sb"], env["wd_sb"]
    HID, TC, TT, IT, NI, FC = (env["HID"], env["TC"], env["TT"], env["IT"],
                               env["NI"], env["FC"])
    T.wait((SC.s, SC.s.v))
    gu_tok, q_tok, ifm_tok, dn_tok, xev_tok = {}, {}, {}, {}, {}
    silu_tok = {}
    for it in range(NI):
        bb = it % 2
        if it + 1 < NI:
            if it >= 1:
                sync.wait_ge(T.s.h, dn_tok[(it - 1, TT - 1)][1])
            load_w(it + 1)
        for t in wtok[it]:
            T.wait(t)
        for tt in range(TT + 1):
            if tt < TT:
                tag = "V" if tt % 2 == 0 else "G"
                ec = EN[tag]
                z = zz[tag]
                prev = (it, tt - 1) if tt >= 1 else (it - 1, TT - 1)
                if prev in silu_tok:
                    T.wait(silu_tok[prev])
                    T.wait(q_tok[prev][2])
                for fc in range(FC):
                    T.s.inc(T.e.matmul(
                        pg[:, :], xfm[:, fc * TC + tt * 128:fc * TC + (tt + 1) * 128],
                        wg_sb[bb][:, fc * IT:(fc + 1) * IT],
                        start=(fc == 0), stop=(fc == FC - 1)))
                for fc in range(FC):
                    T.s.inc(T.e.matmul(
                        pu[:, :], xfm[:, fc * TC + tt * 128:fc * TC + (tt + 1) * 128],
                        wu_sb[bb][:, fc * IT:(fc + 1) * IT],
                        start=(fc == 0), stop=(fc == FC - 1)))
                gu_tok[(it, tt)] = T.tok()
                SC.wait(gu_tok[(it, tt)])
                silu_tok[(it, tt)] = (SC.s, SC.mark(
                    SC.e.activation(z["z0"][:, :], pg[:, :], AF.Silu)))
                mul_ec = ec if tag == "V" else pr.V
                mul_ec.wait(gu_tok[(it, tt)])
                mul_ec.wait(silu_tok[(it, tt)])
                if (it, tt - 2) in ifm_tok:
                    mul_ec.wait(ifm_tok[(it, tt - 2)])
                elif it >= 1 and (it - 1, tt + TT - 2) in ifm_tok:
                    mul_ec.wait(ifm_tok[(it - 1, tt + TT - 2)])
                if tag != "V":
                    mul_ec.wait(ec.tok())
                tm = mul_ec.mark(mul_ec.e.tensor_tensor(z["z0"][:, :], pu[:, :],
                                                        z["z0"][:, :], op=ALU.mult))
                ec.wait((mul_ec.s, tm))
                up_done = (mul_ec.s, tm)
                qt = emit_quant(ec, pr.G, SC, z, z["z0"][:, :], interq[tag][:, :],
                                IT, "act", red=pr.V, src_tok=up_done)
                q_tok[(it, tt)] = (qt[0], qt[1], up_done)
            pt = tt - 1
            if pt >= 0:
                tag = "V" if pt % 2 == 0 else "G"
                T.wait(q_tok[(it, pt)][:2])
                srcs = [interq[tag][:, ic * 128:(ic + 1) * 128]
                        for ic in range(IT // 128)]
                dsts = [ifm[tag][:, ic * 128:(ic + 1) * 128]
                        for ic in range(IT // 128)]
                ifm_tok[(it, pt)] = emit_transpose(pr, srcs, ptr, dsts)
                T.wait(ifm_tok[(it, pt)])
                if (it, pt - 1) in xev_tok:
                    T.wait(xev_tok[(it, pt - 1)])
                elif it >= 1 and (it - 1, TT - 1) in xev_tok:
                    T.wait(xev_tok[(it - 1, TT - 1)])
                nic = IT // 128
                for ic in range(nic):
                    for ot in range(HID // 512):
                        T.s.inc(T.e.matmul(
                            pd[:, ot * 512:(ot + 1) * 512],
                            ifm[tag][:, ic * 128:(ic + 1) * 128],
                            wd_sb[bb][:, ic * HID + ot * 512: ic * HID + (ot + 1) * 512],
                            start=(ic == 0), stop=(ic == nic - 1)))
                dn_tok[(it, pt)] = T.tok()
                V.wait(dn_tok[(it, pt)])
                for ot in range(HID // 512):
                    V.s.inc(V.e.scalar_tensor_tensor(
                        x_sb[:, pt * HID + ot * 512: pt * HID + (ot + 1) * 512],
                        pd[:, ot * 512:(ot + 1) * 512], 0.5,
                        x_sb[:, pt * HID + ot * 512: pt * HID + (ot + 1) * 512],
                        op0=ALU.mult, op1=ALU.add))
                xev_tok[(it, pt)] = V.tok()
    return gu_tok, dn_tok, xev_tok



def build_l1(cfg, TC):
    HID, I, QLR, KVLR, DR, H = (cfg["HID"], cfg["I"], cfg["QLR"], cfg["KVLR"],
                                cfg["DR"], cfg["H"])
    DN = cfg["DN"]
    DQ = DN + DR
    HDQ = H * DQ
    TT, FC, QC = TC // 128, HID // 128, QLR // 128
    IT = 256
    NI = I // IT
    KD = KVLR + DR
    NQP = 4
    PARTC = HDQ // NQP
    NH = H // NQP

    nc = bass.Bass("TRN2", target_bir_lowering=False, debug=False)
    x_h = nc.dram_tensor("x", [TC, HID], F32, kind="ExternalInput").ap()
    wgq_h = nc.dram_tensor("wgq", [NI, 128, FC * IT], BF16, kind="ExternalInput").ap()
    wuq_h = nc.dram_tensor("wuq", [NI, 128, FC * IT], BF16, kind="ExternalInput").ap()
    wdq_h = nc.dram_tensor("wdq", [NI, 128, (IT // 128) * HID], BF16, kind="ExternalInput").ap()
    wqa_h = nc.dram_tensor("wqa", [2, 128, (FC // 2) * QLR], F32, kind="ExternalInput").ap()
    wkva_h = nc.dram_tensor("wkva", [128, FC * KD], F32, kind="ExternalInput").ap()
    wqb_h = nc.dram_tensor("wqb", [NQP, 128, QC * PARTC], F32, kind="ExternalInput").ap()
    qgc_h = nc.dram_tensor("qgc", [128, QC], F32, kind="ExternalInput").ap()
    x2_h = nc.dram_tensor("x2", [TC, HID], F32, kind="ExternalOutput").ap()
    qr_h = nc.dram_tensor("qr", [TC, HDQ], BF16, kind="ExternalOutput").ap()
    ckvl_h = nc.dram_tensor("ckvl", [TC, KVLR], BF16, kind="ExternalOutput").ap()
    kpe_h = nc.dram_tensor("kpe", [TC, DR], BF16, kind="ExternalOutput").ap()

    AR_F32 = (2 * FC * IT + (IT // 128) * HID) // 2
    es = contextlib.ExitStack()
    with es:
        pr = Prog(nc, es)
        x_sb = pr.sb("x_sb", [128, TT * HID])
        xfm = pr.sb("xfm", [128, FC * TC], BF16)
        waren = [pr.sb(f"waren{i}", [128, AR_F32]) for i in range(2)]
        lnout = pr.sb("lnout", [128, max(QLR, KD)])
        zz = {t: quant_scratch(pr, "z" + t, IT) for t in ("V", "G")}
        interq = {t: pr.sb(f"interq{t}", [128, IT], BF16) for t in ("V", "G")}
        ifm = {t: pr.sb(f"ifm{t}", [128, IT], BF16) for t in ("V", "G")}
        cos_t = pr.sb("cos_t", [128, TT * 64])
        sin_t = pr.sb("sin_t", [128, TT * 64])
        trig0 = pr.sb("trig0", [128, max(TT * 64, NH * 64)])
        trig1 = pr.sb("trig1", [128, max(2 * TT * 64, 64 + NH * 64)])
        offvec = pr.sb("offvec", [128, 1])
        lns = {"bn6": pr.sb("bn6", [128, 6 * ((HID + 511) // 512)]),
               "mv": pr.sb("mv", [128, 2]), "r0": pr.sb("r0", [128, 1]),
               "r1": pr.sb("r1", [128, 1])}
        cqn = pr.sb("cqn", [128, QLR], BF16)
        ckvl_st = pr.sb("ckvl_st", [128, KVLR], BF16)
        kpe_st = pr.sb("kpe_st", [128, DR], BF16)
        qr_st = pr.sb("qr_st", [128, PARTC], BF16)
        cqfm = [pr.sb(f"cqfm{qi}", [128, TC], BF16) for qi in range(QC)]
        pg = pr.ps("pg", [128, IT])
        pu = pr.ps("pu", [128, IT])
        pd = pr.ps("pd", [128, HID])
        pq = pr.ps("pq", [128, 512])
        ptr = es.enter_context(nc.psum_tensor("ptr", [128, 512], BF16))

        V, G, SC, T, SY = pr.V, pr.G, pr.SC, pr.T, pr.SY
        EN = {"V": V, "G": G}
        block = es.enter_context(nc.Block())

        @block.sync
        def _(sync):
            itok = pr.make_ident()
            T.wait(itok)
            pr.make_two(IT // 32)
            for arm in nc.gpsimd.switch_core_id(n=128):
                nc.gpsimd.memset(offvec[:, :], float((arm.logical % NCORES) * TC))
            invf = sin_t[:, 0:64]
            iscr = sin_t[:, 64:128]
            G.step(G.e.iota(invf, [[1, 64]], base=0, channel_multiplier=0,
                            allow_small_or_imprecise_dtypes=True))
            V.wait(G.tok())
            V.step(V.e.tensor_scalar(iscr, invf, 31.5, None, op0=ALU.is_gt))
            V.step(V.e.scalar_tensor_tensor(invf, iscr, -32.0, invf,
                                            op0=ALU.mult, op1=ALU.add))
            SC.wait(V.tok())
            th = SC.step(SC.e.activation(invf, invf, AF.Exp,
                                         scale=-(2.0 / DR) * float(np.log(10000.0))))
            G.wait((SC.s, th))
            G.step(G.e.iota(trig0[:, :TT * 64], [[128, TT], [0, 64]], base=0,
                            channel_multiplier=1, allow_small_or_imprecise_dtypes=True))
            G.step(G.e.tensor_scalar(trig0[:, :TT * 64], trig0[:, :TT * 64],
                                     offvec[:, :], None, op0=ALU.add))
            invf3 = invf.unsqueeze(1).broadcast_to([128, TT, 64])
            tr3 = trig0[:, :TT * 64].rearrange("p (t j) -> p t j", j=64)
            G.step(G.e.tensor_tensor(tr3, tr3, invf3, op=ALU.mult))
            C1, C2 = 6.28125, float(2 * np.pi - 6.28125)
            W64 = TT * 64
            u_ap = trig1[:, 0:W64]
            acc_ap = trig1[:, W64:2 * W64]
            for dst, shift in ((sin_t, 0.5), (cos_t, 0.75)):
                G.step(G.e.tensor_scalar(u_ap, trig0[:, :W64],
                                         float(1 / (2 * np.pi)), shift,
                                         op0=ALU.mult, op1=ALU.add))
                V.wait(G.tok())
                V.step(V.e.tensor_scalar(acc_ap, u_ap, 0.0, None, op0=ALU.mult))
                kk = 1024.0
                while kk >= 1.0:
                    V.step(V.e.scalar_tensor_tensor(dst[:, :W64], acc_ap, kk, u_ap,
                                                    op0=ALU.add, op1=ALU.is_le))
                    V.step(V.e.scalar_tensor_tensor(acc_ap, dst[:, :W64], kk, acc_ap,
                                                    op0=ALU.mult, op1=ALU.add))
                    kk /= 2.0
                V.step(V.e.scalar_tensor_tensor(dst[:, :W64], acc_ap, -C1,
                                                trig0[:, :W64], op0=ALU.mult, op1=ALU.add))
                V.step(V.e.scalar_tensor_tensor(dst[:, :W64], acc_ap, -C2,
                                                dst[:, :W64], op0=ALU.mult, op1=ALU.add))
                if shift == 0.75:
                    V.step(V.e.tensor_scalar(dst[:, :W64], dst[:, :W64],
                                             float(np.pi / 2), None, op0=ALU.add))
                SC.wait(V.tok())
                ts_ = SC.step(SC.e.activation(dst[:, :W64], dst[:, :W64], AF.Sin))
                V.wait((SC.s, ts_))
                G.wait((SC.s, ts_))
            trig_tok = (SC.s, SC.s.v)

            xtoks = [pr.load(x_sb[:, tt * HID:(tt + 1) * HID],
                             x_h[tt * 128:(tt + 1) * 128, :], "sy") for tt in range(TT)]

            wg_sb, wu_sb, wd_sb = {}, {}, {}
            for bb in range(2):
                ap = waren[bb].ap().bitcast(BF16)
                wg_sb[bb] = ap[:, 0:FC * IT]
                wu_sb[bb] = ap[:, FC * IT:2 * FC * IT]
                wd_sb[bb] = ap[:, 2 * FC * IT:2 * FC * IT + (IT // 128) * HID]
            wtok = {}

            def load_w(it):
                bb = it % 2
                wtok[it] = (pr.load(wg_sb[bb], wgq_h[it, :, :], "sy"),
                            pr.load(wu_sb[bb], wuq_h[it, :, :], "sy"),
                            pr.load(wd_sb[bb], wdq_h[it, :, :], "sy"))

            load_w(0)

            env = dict(EN=EN, zz=zz, interq=interq, ifm=ifm, lns=lns,
                       x_sb=x_sb[:, :], xfm=xfm[:, :], ptr=ptr,
                       pg=pg, pu=pu, pd=pd, wg_sb=wg_sb, wu_sb=wu_sb, wd_sb=wd_sb,
                       HID=HID, TC=TC, TT=TT, IT=IT, NI=NI, FC=FC, xfm_war=None)
            emit_phase_lnq(pr, env, xtoks, quantize=True)
            gu_tok, dn_tok, xev_tok = emit_phase_ffn(pr, env, sync, load_w, wtok)

            x2_store = []
            for tt in range(TT):
                SY.wait(xev_tok[(NI - 1, tt)])
                x2_store.append(pr.load(x2_h[tt * 128:(tt + 1) * 128, :],
                                        x_sb[:, tt * HID:(tt + 1) * HID], "sy"))

            for tt in range(TT):
                V.wait(xev_tok[(NI - 1, tt)])
            env["xfm_war"] = gu_tok[(NI - 1, TT - 1)]
            emit_phase_lnq(pr, env, None, quantize=False)
            trc_last = (SC.s, SC.s.v)

            nc.gpsimd.wait_ge(T.s.h, dn_tok[(NI - 1, TT - 1)][1])
            wqa_sb = [waren[0].ap().bitcast(BF16)[:, :(FC // 2) * QLR],
                      waren[1].ap().bitcast(BF16)[:, :(FC // 2) * QLR]]
            qa_tok = [pr.load(wqa_sb[hh], wqa_h[hh, :, :], "g") for hh in range(2)]
            for t in x2_store:
                V.wait(t)
            junk = x_sb
            pq_ev = None
            cq_tr = {}
            for tt in range(TT):
                T.wait(trc_last)
                for t in qa_tok:
                    T.wait(t)
                for qt in range(QLR // 512):
                    if pq_ev is not None:
                        T.wait(pq_ev)
                    for fc in range(FC):
                        rhs = wqa_sb[fc // (FC // 2)][:, (fc % (FC // 2)) * QLR + qt * 512:
                                                      (fc % (FC // 2)) * QLR + (qt + 1) * 512]
                        T.s.inc(T.e.matmul(
                            pq[:, :], xfm[:, fc * TC + tt * 128:fc * TC + (tt + 1) * 128],
                            rhs, start=(fc == 0), stop=(fc == FC - 1)))
                    V.wait(T.tok())
                    V.s.inc(V.e.tensor_copy(lnout[:, qt * 512:(qt + 1) * 512], pq[:, :]))
                    pq_ev = V.tok()
                V.e.wait_ge(V.s.h, V.s.v)
                rt = emit_rms_rstd(pr, lnout[:, :QLR], QLR, lns, junk[:, :QLR])
                V.wait((V.s, rt))
                V.step(V.e.tensor_scalar(cqn[:, :], lnout[:, :QLR], lns["r0"][:, :],
                                         None, op0=ALU.mult))
                T.wait(V.tok())
                srcs = [cqn[:, qc * 128:(qc + 1) * 128] for qc in range(QC)]
                dsts = [cqfm[qc][:, tt * 128:(tt + 1) * 128] for qc in range(QC)]
                cq_tr[tt] = emit_transpose(pr, srcs, ptr, dsts, evac=V)

            nc.gpsimd.wait_ge(T.s.h, T.s.v)
            wkva_sb = waren[0].ap().bitcast(BF16)[:, :FC * KD]
            kva_tok = pr.load(wkva_sb, wkva_h[:, :], "g")
            ck_store = kp_store = None
            for tt in range(TT):
                T.wait(kva_tok)
                c0 = 0
                while c0 < KD:
                    cw = min(512, KD - c0)
                    if pq_ev is not None:
                        T.wait(pq_ev)
                    for fc in range(FC):
                        T.s.inc(T.e.matmul(
                            pq[:, :cw], xfm[:, fc * TC + tt * 128:fc * TC + (tt + 1) * 128],
                            wkva_sb[:, fc * KD + c0: fc * KD + c0 + cw],
                            start=(fc == 0), stop=(fc == FC - 1)))
                    V.wait(T.tok())
                    V.s.inc(V.e.tensor_copy(lnout[:, c0:c0 + cw], pq[:, :cw]))
                    pq_ev = V.tok()
                    c0 += cw
                V.e.wait_ge(V.s.h, V.s.v)
                rt = emit_rms_rstd(pr, lnout[:, :KVLR], KVLR, lns, junk[:, :KVLR])
                V.wait((V.s, rt))
                if ck_store is not None:
                    V.wait(ck_store)
                V.step(V.e.tensor_scalar(ckvl_st[:, :], lnout[:, :KVLR], lns["r0"][:, :],
                                         None, op0=ALU.mult))
                SY.wait(V.tok())
                ck_store = pr.load(ckvl_h[tt * 128:(tt + 1) * 128, :], ckvl_st[:, :], "sy")
                pe = lnout[:, KVLR:KD]
                cs = cos_t[:, tt * 64:(tt + 1) * 64]
                sn = sin_t[:, tt * 64:(tt + 1) * 64]
                V.step(V.e.tensor_tensor(trig1[:, 0:64], pe, cs, op=ALU.mult))
                V.step(V.e.tensor_tensor(trig1[:, 64:96], pe[:, 32:64], sn[:, 0:32], op=ALU.mult))
                V.step(V.e.tensor_tensor(trig1[:, 96:128], pe[:, 0:32], sn[:, 32:64], op=ALU.mult))
                if kp_store is not None:
                    V.wait(kp_store)
                V.step(V.e.tensor_tensor(kpe_st[:, 0:32], trig1[:, 0:32], trig1[:, 64:96],
                                         op=ALU.subtract))
                V.step(V.e.tensor_tensor(kpe_st[:, 32:64], trig1[:, 32:64], trig1[:, 96:128],
                                         op=ALU.add))
                SY.wait(V.tok())
                kp_store = pr.load(kpe_h[tt * 128:(tt + 1) * 128, :], kpe_st[:, :], "sy")

            nc.sync.wait_ge(V.s.h, V.s.v)
            gq_tok = pr.load(trig1[:, 0:QC], qgc_h[:, :], "sy")
            qr_store = None
            for pp in range(NQP):
                nc.gpsimd.wait_ge(T.s.h, T.s.v)
                nc.gpsimd.wait_ge(V.s.h, V.s.v)
                wqb_sb = waren[0].ap().bitcast(BF16)[:, :QC * PARTC]
                ptile = waren[1].ap()[:, :PARTC]
                qb_tok = pr.load(wqb_sb, wqb_h[pp, :, :], "g")
                G.wait(qb_tok)
                G.wait(gq_tok)
                for qc in range(QC):
                    G.step(G.e.tensor_scalar(wqb_sb[:, qc * PARTC:(qc + 1) * PARTC],
                                             wqb_sb[:, qc * PARTC:(qc + 1) * PARTC],
                                             trig1[:, qc:qc + 1], None, op0=ALU.mult))
                wqb_ready = G.tok()
                T.wait(wqb_ready)
                for tt in range(TT):
                    T.wait(cq_tr[tt])
                    c0 = 0
                    while c0 < PARTC:
                        cw = min(512, PARTC - c0)
                        if pq_ev is not None:
                            T.wait(pq_ev)
                        for qc in range(QC):
                            T.s.inc(T.e.matmul(
                                pq[:, :cw], cqfm[qc][:, tt * 128:(tt + 1) * 128],
                                wqb_sb[:, qc * PARTC + c0: qc * PARTC + c0 + cw],
                                start=(qc == 0), stop=(qc == QC - 1)))
                        V.wait(T.tok())
                        V.s.inc(V.e.tensor_copy(ptile[:, c0:c0 + cw], pq[:, :cw]))
                        pq_ev = V.tok()
                        c0 += cw
                    V.e.wait_ge(V.s.h, V.s.v)
                    pt3 = ptile[:, :NH * DQ].rearrange("p (h d) -> p h d", d=DQ)
                    qr3 = qr_st[:, :].rearrange("p (h d) -> p h d", d=DQ)
                    cs3 = cos_t[:, tt * 64:(tt + 1) * 64].unsqueeze(1).broadcast_to([128, NH, 64])
                    sn3 = sin_t[:, tt * 64:(tt + 1) * 64].unsqueeze(1).broadcast_to([128, NH, 64])
                    t13 = trig0[:, :NH * 64].rearrange("p (h d) -> p h d", d=64)
                    r3 = trig1[:, 64:64 + NH * 64].rearrange("p (h d) -> p h d", d=64)
                    pe3 = pt3[:, :, DN:DQ]
                    if qr_store is not None:
                        V.wait(qr_store)
                    V.step(V.e.tensor_copy(qr3[:, :, 0:DN], pt3[:, :, 0:DN]))
                    V.step(V.e.tensor_tensor(t13, pe3, cs3, op=ALU.mult))
                    V.step(V.e.tensor_tensor(r3[:, :, 0:32], pe3[:, :, 32:64],
                                             sn3[:, :, 0:32], op=ALU.mult))
                    V.step(V.e.tensor_tensor(r3[:, :, 32:64], pe3[:, :, 0:32],
                                             sn3[:, :, 32:64], op=ALU.mult))
                    V.step(V.e.tensor_tensor(qr3[:, :, DN:DN + 32], t13[:, :, 0:32],
                                             r3[:, :, 0:32], op=ALU.subtract))
                    V.step(V.e.tensor_tensor(qr3[:, :, DN + 32:DQ], t13[:, :, 32:64],
                                             r3[:, :, 32:64], op=ALU.add))
                    SY.wait(V.tok())
                    qr_store = pr.load(qr_h[tt * 128:(tt + 1) * 128, pp * PARTC:(pp + 1) * PARTC],
                                       qr_st[:, :], "sy")

            for q in ("sy", "sc", "g"):
                for s in pr.dma_slots[q]:
                    if s.v > 0:
                        sync.wait_ge(s.h, s.v)
            for ecx in (V, G, SC, T):
                sync.wait_ge(ecx.s.h, ecx.s.v)

    return nc



def build_l2(cfg, HPC):
    S, KVLR, DR, DN, DV = cfg["S"], cfg["KVLR"], cfg["DR"], cfg["DN"], cfg["DV"]
    DQ = DN + DR
    KC = KVLR // 128
    ST = S // 128
    NKV = DN + DV
    SCALE = float(DQ ** -0.5)
    CLAMP_OFF = 87.0 / SCALE

    nc = bass.Bass("TRN2", target_bir_lowering=False, debug=False)
    q4_h = nc.dram_tensor("q4", [HPC, 128, ST * DQ], BF16, kind="ExternalInput").ap()
    ckvl_h = nc.dram_tensor("ckvl", [S, KVLR], BF16, kind="ExternalInput").ap()
    kpe_h = nc.dram_tensor("kpe", [S, DR], BF16, kind="ExternalInput").ap()
    wkvb_h = nc.dram_tensor("wkvb", [128, KC * HPC * NKV], F32, kind="ExternalInput").ap()
    kgc_h = nc.dram_tensor("kgc", [128, KC], F32, kind="ExternalInput").ap()
    o4_h = nc.dram_tensor("o4", [HPC * DV, S], BF16, kind="ExternalOutput").ap()

    es = contextlib.ExitStack()
    with es:
        pr = Prog(nc, es)
        ckv_fm = pr.sb("ckv_fm", [128, KC * S], BF16)
        kpe_fm = pr.sb("kpe_fm", [128, S], BF16)
        stg = [pr.sb(f"stg{i}", [128, max(KVLR, DQ)], BF16) for i in range(2)]
        qst = [pr.sb(f"qst{i}", [128, ST * DQ], BF16) for i in range(2)]
        q_fm = pr.sb("q_fm", [128, S], BF16)
        qpe_fm = pr.sb("qpe_fm", [128, S], BF16)
        k_fm = pr.sb("k_fm", [128, S], BF16)
        v_fm = pr.sb("v_fm", [128, S], BF16)
        v_tm = pr.sb("v_tm", [128, ST * DV], BF16)
        wkvb_sb = pr.sb("wkvb_sb", [128, KC * HPC * NKV], BF16)
        srow = pr.sb("srow", [128, S])
        probs = [pr.sb(f"probs{i}", [128, S], BF16) for i in range(4)]
        pts = pr.sb("pts", [128, 512], BF16)
        o_sb = pr.sb("o_sb", [128, S], BF16)
        msk = pr.sb("msk", [128, 128])
        nm = pr.sb("nm", [128, 1])
        lo = pr.sb("lo", [128, 1])
        sums = pr.sb("sums", [128, 1])
        rs = pr.sb("rs", [128, 1])
        psc = pr.ps("psc", [128, 512])
        po = pr.ps("po", [128, 512])
        pkv = pr.ps("pkv", [128, 512])
        ptr = es.enter_context(nc.psum_tensor("ptr", [128, 512], BF16))

        V, G, SC, T, SY = pr.V, pr.G, pr.SC, pr.T, pr.SY
        block = es.enter_context(nc.Block())

        @block.sync
        def _(sync):
            itok = pr.make_ident()
            T.wait(itok)
            G.step(G.e.memset(msk[:, :], 0.0))
            G.step(G.e.affine_select(msk[:, :], msk[:, :], [[-1, 128]], ALU.is_ge,
                                     NEG_BIG, base=0, channel_multiplier=1))
            mask_tok = G.tok()
            V.wait(mask_tok)

            wk_tok = pr.load(wkvb_sb[:, :], wkvb_h[:, :], "g")
            kgc_sb = srow
            kg_tok = pr.load(kgc_sb[:, 0:KC], kgc_h[:, :], "sy")
            G.wait(wk_tok)
            G.wait(kg_tok)
            for kc in range(KC):
                G.step(G.e.tensor_scalar(wkvb_sb[:, kc * HPC * NKV:(kc + 1) * HPC * NKV],
                                         wkvb_sb[:, kc * HPC * NKV:(kc + 1) * HPC * NKV],
                                         kgc_sb[:, kc:kc + 1], None, op0=ALU.mult))
            wkvb_ready = G.tok()
            T.wait(wkvb_ready)

            ld_prev = {}
            for st in range(ST):
                b = st % 2
                if (st - 2) in ld_prev:
                    pass
                ct = pr.load(stg[b][:, :KVLR], ckvl_h[st * 128:(st + 1) * 128, :], "sy")
                T.wait(ct)
                srcs = [stg[b][:, kc * 128:(kc + 1) * 128] for kc in range(KC)]
                dsts = [ckv_fm[:, kc * S + st * 128: kc * S + (st + 1) * 128] for kc in range(KC)]
                tr = emit_transpose(pr, srcs, ptr, dsts)
                SY.wait(tr)
                ld_prev[st] = tr
            for st in range(ST):
                b = st % 2
                kt = pr.load(stg[b][:, :DR], kpe_h[st * 128:(st + 1) * 128, :], "sy")
                T.wait(kt)
                T.wait(SC.tok())
                T.s.inc(T.e.transpose(ptr[0:DR, 0:128], stg[b][:, :DR], pr.ident[:, :]))
                SC.wait(T.tok())
                SC.s.inc(SC.e.copy(kpe_fm[0:DR, st * 128:(st + 1) * 128], ptr[0:DR, 0:128]))
                SY.wait(SC.tok())
            kpe_ready = SC.tok()

            qld = {0: pr.load(qst[0][:, :], q4_h[0, :, :], "sy")}
            o_store = None
            for h in range(HPC):
                if h + 1 < HPC:
                    qld[h + 1] = pr.load(qst[(h + 1) % 2][:, :], q4_h[h + 1, :, :], "sy")
                qb = qst[h % 2]
                T.wait(qld[h])
                for st in range(ST):
                    T.wait(SC.tok())
                    T.s.inc(T.e.transpose(ptr[:, 0:128], qb[:, st * DQ: st * DQ + DN],
                                          pr.ident[:, :]))
                    T.s.inc(T.e.transpose(ptr[0:DR, 128:256], qb[:, st * DQ + DN: (st + 1) * DQ],
                                          pr.ident[:, :]))
                    SC.wait(T.tok())
                    SC.s.inc(SC.e.copy(q_fm[:, st * 128:(st + 1) * 128], ptr[:, 0:128]))
                    SC.s.inc(SC.e.copy(qpe_fm[0:DR, st * 128:(st + 1) * 128], ptr[0:DR, 128:256]))
                q_ready = SC.tok()
                T.wait(q_ready)
                T.wait(kpe_ready)

                kv_ev = None
                for part, dst in ((0, k_fm), (1, v_fm)):
                    for sc0 in range(S // 512):
                        if kv_ev is not None:
                            T.wait(kv_ev)
                        for kc in range(KC):
                            lhs = wkvb_sb[:, kc * HPC * NKV + h * NKV + part * DN:
                                          kc * HPC * NKV + h * NKV + part * DN + 128]
                            T.s.inc(T.e.matmul(pkv[:, :], lhs,
                                               ckv_fm[:, kc * S + sc0 * 512:kc * S + (sc0 + 1) * 512],
                                               start=(kc == 0), stop=(kc == KC - 1)))
                        SC.wait(T.tok())
                        SC.s.inc(SC.e.copy(dst[:, sc0 * 512:(sc0 + 1) * 512], pkv[:, :]))
                        kv_ev = SC.tok()
                T.wait(kv_ev)
                for st in range(ST):
                    T.wait(SC.tok())
                    T.s.inc(T.e.transpose(ptr[:, 0:128], v_fm[:, st * 128:(st + 1) * 128],
                                          pr.ident[:, :]))
                    SC.wait(T.tok())
                    SC.s.inc(SC.e.copy(v_tm[:, st * 128:(st + 1) * 128], ptr[:, 0:128]))
                v_ready = SC.tok()

                grp_tok = None
                sc_ev = []
                srow_free = None
                for qg in range(ST // 4):
                    ktiles = qg + 1
                    pb_tok = []
                    for qi in range(4):
                        qt = qg * 4 + qi
                        kn = (qt + 1) * 128
                        nk5 = (kn + 511) // 512
                        for kt in range(nk5):
                            cw = min(512, kn - kt * 512)
                            for t in sc_ev:
                                T.wait(t)
                            T.s.inc(T.e.matmul(psc[:, :cw], q_fm[:, qt * 128:(qt + 1) * 128],
                                               k_fm[:, kt * 512:kt * 512 + cw],
                                               start=True, stop=False))
                            T.s.inc(T.e.matmul(psc[:, :cw], qpe_fm[0:DR, qt * 128:(qt + 1) * 128],
                                               kpe_fm[0:DR, kt * 512:kt * 512 + cw],
                                               start=False, stop=True))
                            smm = T.tok()
                            d0 = qt * 128 - kt * 512
                            sc_ev = []
                            if 0 <= d0 < cw:
                                if d0 > 0:
                                    SC.wait(smm)
                                    if srow_free is not None:
                                        SC.wait(srow_free)
                                    SC.s.inc(SC.e.copy(srow[:, kt * 512:kt * 512 + d0],
                                                       psc[:, :d0]))
                                    sc_ev.append((SC.s, SC.s.v))
                                V.wait(smm)
                                V.s.inc(V.e.tensor_tensor(srow[:, qt * 128:qt * 128 + 128],
                                                          psc[:, d0:d0 + 128], msk[:, :],
                                                          op=ALU.add))
                                sc_ev.append((V.s, V.s.v))
                            else:
                                SC.wait(smm)
                                if srow_free is not None:
                                    SC.wait(srow_free)
                                SC.s.inc(SC.e.copy(srow[:, kt * 512:kt * 512 + cw], psc[:, :cw]))
                                sc_ev.append((SC.s, SC.s.v))
                        for t in sc_ev:
                            V.wait(t)
                        V.wait((SC.s, SC.s.v))
                        V.e.wait_ge(V.s.h, V.s.v)
                        V.step(V.e.tensor_reduce(nm[:, :], srow[:, :kn], axis=AX.X,
                                                 op=ALU.max, negate=True))
                        V.step(V.e.tensor_scalar(lo[:, :], nm[:, :], -1.0, -CLAMP_OFF,
                                                 op0=ALU.mult, op1=ALU.add))
                        V.step(V.e.tensor_scalar(nm[:, :], nm[:, :], SCALE, None, op0=ALU.mult))
                        V.step(V.e.tensor_scalar(srow[:, :kn], srow[:, :kn], lo[:, :],
                                                 None, op0=ALU.max))
                        SC.wait(V.tok())
                        et = SC.step(SC.e.activation(srow[:, :kn], srow[:, :kn], AF.Exp,
                                                     bias=nm[:, :], scale=SCALE,
                                                     accum_out=sums[:, :]))
                        V.wait((SC.s, et))
                        V.step(V.e.reciprocal(rs[:, :], sums[:, :]))
                        if grp_tok is not None:
                            V.wait(grp_tok)
                        V.step(V.e.tensor_scalar(probs[qi][:, :kn], srow[:, :kn],
                                                 rs[:, :], None, op0=ALU.mult))
                        srow_free = V.tok()
                        if kn < ktiles * 512:
                            V.step(V.e.memset(probs[qi][:, kn:ktiles * 512], 0.0))
                        pb_tok.append(V.tok())
                    for t in pb_tok:
                        T.wait(t)
                    T.wait(v_ready)
                    for kt in range(ktiles * 4):
                        T.wait(SC.tok())
                        for qi in range(4):
                            T.s.inc(T.e.transpose(ptr[:, qi * 128:(qi + 1) * 128],
                                                  probs[qi][:, kt * 128:(kt + 1) * 128],
                                                  pr.ident[:, :]))
                        SC.wait(T.tok())
                        SC.s.inc(SC.e.copy(pts[:, :], ptr[:, :]))
                        T.wait(SC.tok())
                        T.s.inc(T.e.matmul(po[:, :], v_tm[:, kt * 128:(kt + 1) * 128],
                                           pts[:, :], start=(kt == 0),
                                           stop=(kt == ktiles * 4 - 1)))
                    grp_tok = T.tok()
                    SC.wait(grp_tok)
                    if o_store is not None and qg == 0:
                        SC.wait(o_store)
                    SC.s.inc(SC.e.copy(o_sb[:, qg * 512:(qg + 1) * 512], po[:, :]))
                SY.wait(SC.tok())
                o_store = pr.load(o4_h[h * DV:(h + 1) * DV, :], o_sb[:, :], "sy")

            for q in ("sy", "sc", "g"):
                for s_ in pr.dma_slots[q]:
                    if s_.v > 0:
                        sync.wait_ge(s_.h, s_.v)
            for ecx in (V, G, SC, T):
                sync.wait_ge(ecx.s.h, ecx.s.v)

    return nc




def build_l3(cfg, TC):
    HID, I, H, DV = cfg["HID"], cfg["I"], cfg["H"], cfg["DV"]
    TT, FC = TC // 128, HID // 128
    IT = 256
    NI = I // IT
    OC = (H * DV) // 128
    NWQ = 4
    OCQ = OC // NWQ

    nc = bass.Bass("TRN2", target_bir_lowering=False, debug=False)
    x2_h = nc.dram_tensor("x2", [TC, HID], F32, kind="ExternalInput").ap()
    ofm_h = nc.dram_tensor("ofm", [128, OC * TC], BF16, kind="ExternalInput").ap()
    wo_h = nc.dram_tensor("wo", [NWQ, 128, OCQ * HID], F32, kind="ExternalInput").ap()
    wgq_h = nc.dram_tensor("wgq", [NI, 128, FC * IT], BF16, kind="ExternalInput").ap()
    wuq_h = nc.dram_tensor("wuq", [NI, 128, FC * IT], BF16, kind="ExternalInput").ap()
    wdq_h = nc.dram_tensor("wdq", [NI, 128, (IT // 128) * HID], BF16, kind="ExternalInput").ap()
    out_h = nc.dram_tensor("out", [TC, HID], F32, kind="ExternalOutput").ap()

    AR_F32 = (2 * FC * IT + (IT // 128) * HID) // 2
    assert OCQ * HID * 2 <= AR_F32 * 4, "wo quarter must fit an arena"
    es = contextlib.ExitStack()
    with es:
        pr = Prog(nc, es)
        x_sb = pr.sb("x_sb", [128, TT * HID])
        xfm = pr.sb("xfm", [128, max(FC, OC) * TC], BF16)
        waren = [pr.sb(f"waren{i}", [128, AR_F32]) for i in range(2)]
        zz = {t: quant_scratch(pr, "z" + t, IT) for t in ("V", "G")}
        interq = {t: pr.sb(f"interq{t}", [128, IT], BF16) for t in ("V", "G")}
        ifm = {t: pr.sb(f"ifm{t}", [128, IT], BF16) for t in ("V", "G")}
        lns = {"bn6": pr.sb("bn6", [128, 6 * ((HID + 511) // 512)]),
               "mv": pr.sb("mv", [128, 2]), "r0": pr.sb("r0", [128, 1]),
               "r1": pr.sb("r1", [128, 1])}
        pg = pr.ps("pg", [128, IT])
        pu = pr.ps("pu", [128, IT])
        pd = pr.ps("pd", [128, HID])
        ptr = es.enter_context(nc.psum_tensor("ptr", [128, 512], BF16))

        V, G, SC, T, SY = pr.V, pr.G, pr.SC, pr.T, pr.SY
        EN = {"V": V, "G": G}
        block = es.enter_context(nc.Block())

        @block.sync
        def _(sync):
            itok = pr.make_ident()
            T.wait(itok)
            pr.make_two(IT // 32)
            xtoks = [pr.load(x_sb[:, tt * HID:(tt + 1) * HID],
                             x2_h[tt * 128:(tt + 1) * 128, :], "sy") for tt in range(TT)]
            of_tok = pr.load(xfm[:, 0:OC * TC], ofm_h[:, :], "sy")

            wo_sb = [waren[b].ap().bitcast(BF16)[:, :OCQ * HID] for b in range(2)]
            wq_tok = {0: pr.load(wo_sb[0], wo_h[0, :, :], "g")}
            xev = {}
            for qq in range(NWQ):
                bb = qq % 2
                if qq + 1 < NWQ:
                    if qq >= 1:
                        nc.gpsimd.wait_ge(T.s.h, T.s.v)
                    wq_tok[qq + 1] = pr.load(wo_sb[(qq + 1) % 2], wo_h[qq + 1, :, :], "g")
                T.wait(wq_tok[qq])
                T.wait(of_tok)
                for tt in range(TT):
                    if (qq, tt - 1) in xev:
                        T.wait(xev[(qq, tt - 1)])
                    elif qq >= 1 and (qq - 1, TT - 1) in xev:
                        T.wait(xev[(qq - 1, TT - 1)])
                    for oc4 in range(OCQ):
                        oc = qq * OCQ + oc4
                        for ot in range(HID // 512):
                            T.s.inc(T.e.matmul(
                                pd[:, ot * 512:(ot + 1) * 512],
                                xfm[:, oc * TC + tt * 128:oc * TC + (tt + 1) * 128],
                                wo_sb[bb][:, oc4 * HID + ot * 512: oc4 * HID + (ot + 1) * 512],
                                start=(oc4 == 0), stop=(oc4 == OCQ - 1)))
                    wt = T.tok()
                    V.wait(wt)
                    V.wait(xtoks[tt])
                    for ot in range(HID // 512):
                        V.s.inc(V.e.scalar_tensor_tensor(
                            x_sb[:, tt * HID + ot * 512: tt * HID + (ot + 1) * 512],
                            pd[:, ot * 512:(ot + 1) * 512], 1.0,
                            x_sb[:, tt * HID + ot * 512: tt * HID + (ot + 1) * 512],
                            op0=ALU.mult, op1=ALU.add))
                    xev[(qq, tt)] = V.tok()
            wo_done = [xev[(NWQ - 1, tt)] for tt in range(TT)]

            wg_sb, wu_sb, wd_sb = {}, {}, {}
            for bb in range(2):
                ap = waren[bb].ap().bitcast(BF16)
                wg_sb[bb] = ap[:, 0:FC * IT]
                wu_sb[bb] = ap[:, FC * IT:2 * FC * IT]
                wd_sb[bb] = ap[:, 2 * FC * IT:2 * FC * IT + (IT // 128) * HID]
            wtok = {}

            def load_w(it):
                bb = it % 2
                wtok[it] = (pr.load(wg_sb[bb], wgq_h[it, :, :], "sy"),
                            pr.load(wu_sb[bb], wuq_h[it, :, :], "sy"),
                            pr.load(wd_sb[bb], wdq_h[it, :, :], "sy"))

            sync.wait_ge(T.s.h, T.s.v)
            load_w(0)

            env = dict(EN=EN, zz=zz, interq=interq, ifm=ifm, lns=lns,
                       x_sb=x_sb[:, :], xfm=xfm[:, :], ptr=ptr,
                       pg=pg, pu=pu, pd=pd, wg_sb=wg_sb, wu_sb=wu_sb, wd_sb=wd_sb,
                       HID=HID, TC=TC, TT=TT, IT=IT, NI=NI, FC=FC,
                       xfm_war=(T.s, T.s.v))
            emit_phase_lnq(pr, env, wo_done, quantize=True)
            gu_tok, dn_tok, xev_tok = emit_phase_ffn(pr, env, sync, load_w, wtok)

            for tt in range(TT):
                SY.wait(xev_tok[(NI - 1, tt)])
                pr.load(out_h[tt * 128:(tt + 1) * 128, :],
                        x_sb[:, tt * HID:(tt + 1) * HID], "sy")

            for q in ("sy", "sc", "g"):
                for s_ in pr.dma_slots[q]:
                    if s_.v > 0:
                        sync.wait_ge(s_.h, s_.v)
            for ecx in (V, G, SC, T):
                sync.wait_ge(ecx.s.h, ecx.s.v)

    return nc


def _split_rows(a, n):
    return [np.ascontiguousarray(x) for x in np.split(a, n, axis=0)]


def _layout_ffn_w(wq_T, NI, IT, FC):
    HID = wq_T.shape[0]
    return np.ascontiguousarray(
        wq_T.reshape(FC, 128, NI, IT).transpose(2, 1, 0, 3).reshape(NI, 128, FC * IT))


def kernel(L0_attn_hidden_states, L0_attn_w_qa, L0_attn_q_a_ln_w, L0_attn_w_qb,
           L0_attn_w_kva, L0_attn_kv_a_ln_w, L0_attn_w_kvb, L0_attn_w_o,
           L0_ffn_w_gate, L0_ffn_w_up, L0_ffn_w_down):
    cfg = CFG
    B, S, HID, I = cfg["B"], cfg["S"], cfg["HID"], cfg["I"]
    QLR, KVLR, DR, H, DN, DV = (cfg["QLR"], cfg["KVLR"], cfg["DR"], cfg["H"],
                                cfg["DN"], cfg["DV"])
    DQ = DN + DR
    HDQ = H * DQ
    TC = B * S // NCORES
    TT, FC, QC, KC = TC // 128, HID // 128, QLR // 128, KVLR // 128
    IT = 256
    NI = I // IT
    KD = KVLR + DR
    NQP = 4
    PARTC = HDQ // NQP
    HPC = H // (NCORES // B)
    ST = S // 128
    OC = H * DV // 128
    NWQ = 4
    OCQ = OC // NWQ
    cores = list(range(NCORES))

    wg = np.ascontiguousarray(np.asarray(L0_ffn_w_gate, np.float32))
    wu = np.ascontiguousarray(np.asarray(L0_ffn_w_up, np.float32))
    wd = np.ascontiguousarray(np.asarray(L0_ffn_w_down, np.float32))
    nc0 = build_l0(wg.shape[0] // NCORES, wd.shape[0] // NCORES, HID, I)
    in0 = [{"wg": a, "wu": b, "wd": c}
           for a, b, c in zip(_split_rows(wg, NCORES), _split_rows(wu, NCORES),
                              _split_rows(wd, NCORES))]
    r0 = run_bass_kernel_spmd(nc0, in0, core_ids=cores)
    wgq = np.concatenate([r["wgq"] for r in r0.results], axis=0)
    wuq = np.concatenate([r["wuq"] for r in r0.results], axis=0)
    wdq = np.concatenate([r["wdq"] for r in r0.results], axis=0)

    l_wg = _layout_ffn_w(np.ascontiguousarray(wgq.T), NI, IT, FC)
    l_wu = _layout_ffn_w(np.ascontiguousarray(wuq.T), NI, IT, FC)
    wdqT = np.ascontiguousarray(wdq.T)
    l_wd = np.ascontiguousarray(
        wdqT.reshape(NI, IT // 128, 128, HID).transpose(0, 2, 1, 3)
        .reshape(NI, 128, (IT // 128) * HID))
    wqaT = np.ascontiguousarray(np.asarray(L0_attn_w_qa, np.float32).T)
    l_wqa = np.ascontiguousarray(
        wqaT.reshape(2, FC // 2, 128, QLR).transpose(0, 2, 1, 3)
        .reshape(2, 128, (FC // 2) * QLR))
    wkvaT = np.ascontiguousarray(np.asarray(L0_attn_w_kva, np.float32).T)
    l_wkva = np.ascontiguousarray(
        wkvaT.reshape(FC, 128, KD).transpose(1, 0, 2).reshape(128, FC * KD))
    wqbT = np.ascontiguousarray(np.asarray(L0_attn_w_qb, np.float32).T)
    l_wqb = np.ascontiguousarray(
        wqbT.reshape(QC, 128, NQP, PARTC).transpose(2, 1, 0, 3)
        .reshape(NQP, 128, QC * PARTC))
    l_qgc = np.ascontiguousarray(
        np.asarray(L0_attn_q_a_ln_w, np.float32).reshape(QC, 128).T)
    hs = np.ascontiguousarray(np.asarray(L0_attn_hidden_states, np.float32)
                              .reshape(B * S, HID))
    xs = _split_rows(hs, NCORES)
    nc1 = build_l1(cfg, TC)
    in1 = [{"x": xs[c], "wgq": l_wg, "wuq": l_wu, "wdq": l_wd, "wqa": l_wqa,
            "wkva": l_wkva, "wqb": l_wqb, "qgc": l_qgc} for c in cores]
    r1 = run_bass_kernel_spmd(nc1, in1, core_ids=cores)
    x2 = [r1.results[c]["x2"] for c in cores]
    qr = [r1.results[c]["qr"] for c in cores]
    ckvl = [r1.results[c]["ckvl"] for c in cores]
    kpe = [r1.results[c]["kpe"] for c in cores]

    G4 = NCORES // B
    wkvbT = np.ascontiguousarray(np.asarray(L0_attn_w_kvb, np.float32).T)
    NKV = DN + DV
    l_kgc = np.ascontiguousarray(
        np.asarray(L0_attn_kv_a_ln_w, np.float32).reshape(KC, 128).T)
    in2 = []
    for c in cores:
        b, slot = c // G4, c % G4
        ck_full = np.concatenate([ckvl[b * G4 + j] for j in range(G4)], axis=0)
        kp_full = np.concatenate([kpe[b * G4 + j] for j in range(G4)], axis=0)
        qr_full = np.concatenate([qr[b * G4 + j] for j in range(G4)], axis=0)
        q4 = np.empty((HPC, 128, ST * DQ), dtype=qr_full.dtype)
        for hh in range(HPC):
            gh = slot * HPC + hh
            q4[hh] = np.ascontiguousarray(
                qr_full[:, gh * DQ:(gh + 1) * DQ].reshape(ST, 128, DQ)
                .transpose(1, 0, 2).reshape(128, ST * DQ))
        wsl = wkvbT[:, slot * HPC * NKV:(slot + 1) * HPC * NKV]
        l_wkvb = np.ascontiguousarray(
            wsl.reshape(KC, 128, HPC * NKV).transpose(1, 0, 2)
            .reshape(128, KC * HPC * NKV))
        in2.append({"q4": q4, "ckvl": np.ascontiguousarray(ck_full),
                    "kpe": np.ascontiguousarray(kp_full), "wkvb": l_wkvb,
                    "kgc": l_kgc})
    nc2 = build_l2(cfg, HPC)
    r2 = run_bass_kernel_spmd(nc2, in2, core_ids=cores)
    o4 = [r2.results[c]["o4"] for c in cores]

    woT = np.ascontiguousarray(np.asarray(L0_attn_w_o, np.float32).T)
    l_wo = np.ascontiguousarray(
        woT.reshape(NWQ, OCQ, 128, HID).transpose(0, 2, 1, 3)
        .reshape(NWQ, 128, OCQ * HID))
    in3 = []
    for c in cores:
        b, slot = c // G4, c % G4
        O_fm = np.concatenate([o4[b * G4 + j] for j in range(G4)], axis=0)
        ofm = O_fm[:, slot * TC:(slot + 1) * TC]
        l_ofm = np.ascontiguousarray(
            ofm.reshape(OC, 128, TC).transpose(1, 0, 2).reshape(128, OC * TC))
        in3.append({"x2": x2[c], "ofm": l_ofm, "wo": l_wo, "wgq": l_wg,
                    "wuq": l_wu, "wdq": l_wd})
    nc3 = build_l3(cfg, TC)
    r3 = run_bass_kernel_spmd(nc3, in3, core_ids=cores)
    out = np.concatenate([r3.results[c]["out"] for c in cores], axis=0)
    return np.ascontiguousarray(out.reshape(B, S, HID)).astype(np.float32)
